# version 03
# baseline (speedup 1.0000x reference)
import numpy as np
import ml_dtypes

import concourse.bass as bass
import concourse.mybir as mybir
from concourse.bass_utils import run_bass_kernel_spmd

F32 = mybir.dt.float32
BF16 = mybir.dt.bfloat16
AF = mybir.ActivationFunctionType
ALU = mybir.AluOpType
AX = mybir.AxisListType
ET = mybir.EngineType
bfnp = ml_dtypes.bfloat16

B, S, H = 2, 4096, 4096
NH, NKV, HD, GQ = 32, 8, 128, 4
T = 1024
KT = 1536
DFF = 11008
HHALF = DFF // 2
QL = [1408, 1408, 1408, 1280]
QOFF = [0, 1408, 2816, 4224]
NLC = 11
NDK = 32
NHC = 43
EPS = 1e-5

NF4_CODE = np.array([-1.0, -0.6961928009986877, -0.5250730514526367, -0.39491748809814453,
                     -0.28444138169288635, -0.18477343022823334, -0.09105003625154495, 0.0,
                     0.07958029955625534, 0.16093020141124725, 0.24611230194568634,
                     0.33791524171829224, 0.44070982933044434, 0.5626170039176941,
                     0.7229568362236023, 1.0], dtype=np.float32)
NF4_BOUNDS = ((NF4_CODE[:-1] + NF4_CODE[1:]) * 0.5).astype(np.float32)


def _bf16_chain_consts():
    cb = NF4_CODE.astype(bfnp)
    deltas = []
    for b in range(15):
        d0 = bfnp(np.float32(cb[b + 1]) - np.float32(cb[b]))
        du = d0.view(np.uint16)
        pick = None
        for off in (0, 1, -1, 2, -2):
            d = np.uint16(du + off).view(bfnp)
            if bfnp(np.float32(cb[b]) + np.float32(d)).view(np.uint16) == cb[b + 1].view(np.uint16):
                pick = d
                break
        assert pick is not None
        deltas.append(float(np.float32(pick)))
    return float(np.float32(cb[0])), deltas


CODE0_BF, DELTAS_BF = _bf16_chain_consts()

DQ_DISABLE = False
P1_CHUNKS = 16
DQ_TAP = None
ATTN_TAP = False
LN1_TAP = False
FFN_TAP = False

ENG_OF = {"pe": "tensor", "dve": "vector", "act": "scalar", "pool": "gpsimd"}


class Kb:

    def __init__(self):
        nc = bass.Bass("TRN2", target_bir_lowering=False)
        self.nc = nc
        self.ops = {k: [] for k in ("sync", "scalar", "vector", "tensor", "gpsimd")}
        self.cnt = {k: [nc.alloc_semaphore("m_" + k), 0] for k in ("pe", "dve", "act", "pool")}
        self.cc_sems = []
        self.cc_n = 0
        self.par = {}

    def em(self, eng, fn):
        self.ops[eng].append(fn)

    def mark(self, key):
        s = self.cnt[key]
        s[1] += 1
        sem, val = s[0], s[1]
        eng = ENG_OF[key]
        lst = self.ops[eng]
        if lst and not getattr(lst[-1], "_is_wait", False):
            prev = lst.pop()

            def wrapped(e, prev=prev, s_=sem, v=val):
                r = prev(e)
                if hasattr(r, "then_inc"):
                    r.then_inc(s_, 1)
                else:
                    e.nop().then_inc(s_, 1)
            lst.append(wrapped)
        else:
            self.em(eng, lambda e, s_=sem, v=val: e.nop().then_inc(s_, 1))
        return ("cnt", sem, val)

    def cur(self, key):
        s = self.cnt[key]
        return ("cnt", s[0], s[1])

    def _emw(self, eng, fn):
        fn._is_wait = True
        self.ops[eng].append(fn)

    def wait(self, eng, *gates):
        for g in gates:
            if g is None:
                continue
            kind = g[0]
            if kind == "cnt":
                _, sem, val = g
                if val > 0:
                    self._emw(eng, lambda e, s=sem, v=val: e.wait_ge(s, v))
            elif kind == "cc":
                _, val = g
                for k in range(val):
                    self._emw(eng, lambda e, s=self.cc_sems[k]: e.wait_ge(s, 1))
            elif kind == "dma":
                _, sems, c0, c1 = g
                if c0 > 0:
                    self._emw(eng, lambda e, s=sems[0], v=16 * c0: e.wait_ge(s, v))
                if c1 > 0:
                    self._emw(eng, lambda e, s=sems[1], v=16 * c1: e.wait_ge(s, v))
            else:
                raise ValueError(kind)


class Stream:
    def __init__(self, kb, name, queue):
        self.kb = kb
        self.q = queue
        self.sems = [kb.nc.alloc_semaphore(name + "0"), kb.nc.alloc_semaphore(name + "1")]
        self.c = [0, 0]
        self.k = 0

    def issue(self, mk, gates=()):
        par = self.k % 2
        self.c[par] += 1
        self.kb.wait(self.q, *gates)
        sem = self.sems[par]
        self.kb.em(self.q, lambda e, s=sem, m=mk: m(e).then_inc(s, 16))
        self.k += 1
        return ("dma", self.sems, self.c[0], self.c[1])


def build(upto=9, taps=()):
    kb = Kb()
    nc = kb.nc
    em, mark, curc, wait = kb.em, kb.mark, kb.cur, kb.wait

    def din(name, shape, dt):
        return nc.dram_tensor(name, shape, dt, kind="ExternalInput")

    def dint(name, shape, dt, **kw):
        kind = "ExternalOutput" if name in taps else "Internal"
        return nc.dram_tensor(name, shape, dt, kind=kind, **kw)

    xT = din("xT", [H, KT], F32)
    cosq = din("cosq", [HD, T], BF16)
    sinq = din("sinq", [HD, T], BF16)
    cosk = din("cosk", [HD, KT], BF16)
    sink = din("sink", [HD, KT], BF16)
    masks = din("masks", [8, 128, 640], BF16)
    idbf = din("idbf", [128, 128], BF16)
    idf = din("idf", [128, 128], F32)
    onesf = din("onesf", [128, 128], F32)
    Wq = din("Wq", [32, H, 128], F32)
    Wk = din("Wk", [8, H, 128], F32)
    Wv = din("Wv", [8, H, 128], F32)
    Wo = din("Wo", [32, H, 128], F32)
    w1q = din("w1q", [1408, H], F32)
    w2q = din("w2q", [1408, H], F32)
    w3q = din("w3q", [H, 2048], F32)
    y = nc.dram_tensor("y", [T, H], F32, kind="ExternalOutput")

    slab12 = dint("slab12", [2, NLC, NDK, 128, 128], BF16)
    slab3 = dint("slab3", [NDK, NLC, 128, 128], BF16)
    ghw = {(0, 0): dint("gh10", [4, 6, NDK, 128, 128], BF16),
           (0, 1): dint("gh11", [4, 5, NDK, 128, 128], BF16),
           (1, 0): dint("gh20", [4, 6, NDK, 128, 128], BF16),
           (1, 1): dint("gh21", [4, 5, NDK, 128, 128], BF16)}
    gh3a = dint("gh3a", [4, 16, NLC, 128, 128], BF16)
    gh3b = dint("gh3b", [4, 16, NLC, 128, 128], BF16)
    adr = dint("adr", [8, 32, 128, 128], BF16)
    xmidf = dint("xmidf", [32, 128, T], F32)
    xstage = dint("xstage", [32, 128, T], BF16)
    pstage = dint("pstage", [2, 32, 128, T], F32)
    prstage = dint("prstage", [2, 32, 128, T], F32)
    xsh = nc.dram_tensor("xsh", [2, 32, 128, T], BF16, addr_space="Shared")
    pall = nc.dram_tensor("pall", [2, 2, 32, 128, T], F32, addr_space="Shared")
    tap_k = dint("tap_k", [128, KT], BF16)
    tap_vt = dint("tap_vt", [128, 1024], BF16)
    tap_q = dint("tap_q", [128, 512], BF16)
    tap_xg = dint("tap_xg", [128, 512], BF16)
    tap_wq = dint("tap_wq", [128, 4096], BF16)
    tap_ps = dint("tap_ps", [128, 512], F32)
    tap_p = dint("tap_p", [128, 640], BF16)
    tap_sm = dint("tap_sm", [128, 640], F32)
    tap_m = dint("tap_m", [128, 5120], BF16)
    tap_hact = dint("tap_hact", [43, 128, 512], BF16)
    tap_v = dint("tap_v", [128, 1024], F32)
    tap_in = dint("tap_in", [128, 1024], F32)
    tap_am = dint("tap_am", [128, 64], F32)
    tap_bf = dint("tap_bf", [128, 4096], BF16)
    ccin = nc.dram_tensor("ccin", [1, 4], F32)
    ccout = nc.dram_tensor("ccout", [1, 4], F32)

    GROUPS4 = [[0, 2, 4, 6], [1, 3, 5, 7]]
    PAIRS = [[2 * i, 2 * i + 1] for i in range(4)]

    ident_bf = nc.alloc_sbuf_tensor("ident_bf", [128, 128], BF16)
    ident_f = nc.alloc_sbuf_tensor("ident_f", [128, 128], F32)
    ones_f = nc.alloc_sbuf_tensor("ones_f", [128, 128], F32)
    ARENA = 206 * 1024
    arena = nc.alloc_sbuf_tensor("arena", [128, ARENA], mybir.dt.uint8)

    def carve(off, shape, dt):
        esz = 2 if dt == BF16 else 4
        n = int(np.prod(shape[1:]))
        assert off % 4 == 0
        end = off + n * esz
        assert end <= ARENA, (off, shape, end)
        v = arena[:, off:end].bitcast(dt)
        if len(shape) == 3:
            v = v.rearrange("p (a b) -> p a b", b=shape[2])
        return v, end

    def carve_list(off, count, shape, dt):
        out = []
        for _ in range(count):
            t, off = carve(off, shape, dt)
            out.append(t)
        return out, off

    o = 0
    dq_in0, o = carve(o, [128, 1024], F32)
    dq_in1, o = carve(o, [128, 1024], F32)
    dq_v, o = carve(o, [128, 1024], F32)
    dq_ind, o = carve(o, [128, 1024], BF16)
    dq_a0, o = carve(o, [128, 1024], BF16)
    dq_a1, o = carve(o, [128, 1024], BF16)
    dq_am, o = carve(o, [128, 64], F32)
    dq_am2, o = carve(o, [128, 64], F32)
    dq_rcp, o = carve(o, [128, 64], F32)
    dq_bf, o = carve(o, [128, 4096], BF16)
    dq_ts, o = carve(o, [128, 256], BF16)
    DQ_END = (o + 3) // 4 * 4

    ROPE_OFF = ARENA - (2 * 2 * T + 2 * 2 * KT)
    o = ROPE_OFF
    cosq_sb, o = carve(o, [128, T], BF16)
    sinq_sb, o = carve(o, [128, T], BF16)
    cosk_sb, o = carve(o, [128, KT], BF16)
    sink_sb, o = carve(o, [128, KT], BF16)
    assert o == ARENA

    o = DQ_END
    Kfm, o = carve_list(o, NKV, [128, KT], BF16)
    Vtm, o = carve_list(o, 12, [128, NKV * HD], BF16)
    KV_END = (o + 3) // 4 * 4

    o = KV_END
    xbf, o = carve_list(o, 32, [128, KT], BF16)
    wf0, o = carve(o, [128, 1024], F32)
    wf1, o = carve(o, [128, 1024], F32)
    wbf, o = carve(o, [128, 4096], BF16)
    vfm, o = carve(o, [128, KT], BF16)
    rt1, o = carve(o, [128, 512], F32)
    rt2, o = carve(o, [128, 512], F32)
    assert o <= ROPE_OFF, o

    o = KV_END
    xg, o = carve_list(o, 32, [128, 512], BF16)
    qbuf, o = carve_list(o, 32, [128, 512], BF16)
    wqf0, o = carve(o, [128, 1024], F32)
    wqf1, o = carve(o, [128, 1024], F32)
    wqbf, o = carve(o, [128, 4096], BF16)
    wqbf2, o = carve(o, [128, 4096], BF16)
    q_rt1, o = carve(o, [128, 512], F32)
    q_rt2, o = carve(o, [128, 512], F32)
    mask_sb, o = carve(o, [128, 2 * 640], BF16)
    exp_bf, o = carve(o, [128, 640], BF16)
    pm_bf, o = carve(o, [128, 640], BF16)
    sm_f32, o = carve(o, [128, 640], F32)
    pt_sb, o = carve(o, [128, 640], BF16)
    negmax, o = carve(o, [128, 4], F32)
    ssum, o = carve(o, [128, 64], F32)
    srecip, o = carve(o, [128, 64], F32)
    astage0, o = carve(o, [128, 32 * 128], BF16)
    astage1, o = carve(o, [128, 32 * 128], BF16)
    assert o <= ROPE_OFF, o

    o = DQ_END
    ah, o = carve_list(o, 32, [128, 512], BF16)
    res1, o = carve_list(o, 32, [128, 512], F32)
    wof0, o = carve(o, [128, 1024], F32)
    wof1, o = carve(o, [128, 1024], F32)
    wobf, o = carve(o, [128, 4096], BF16)
    wobf2, o = carve(o, [128, 4096], BF16)
    xr0, o = carve(o, [128, 512], F32)
    xr1, o = carve(o, [128, 512], F32)
    sqt, o = carve(o, [128, 512], F32)
    ln_mean, o = carve(o, [128, 512], F32)
    ln_var, o = carve(o, [128, 512], F32)
    ln_rstd, o = carve(o, [128, 512], F32)
    ln_std, o = carve(o, [128, 512], F32)
    ln_b, o = carve(o, [128, 512], F32)
    xmid_c, o = carve(o, [128, 512], F32)
    assert o <= ROPE_OFF, o

    o = DQ_END
    xmg, o = carve_list(o, 32, [128, 512], BF16)
    hact, o = carve_list(o, NHC, [128, 512], BF16)
    w1t0, o = carve(o, [128, 32 * 128], BF16)
    w1t1, o = carve(o, [128, 32 * 128], BF16)
    w2t0, o = carve(o, [128, 32 * 128], BF16)
    w2t1, o = carve(o, [128, 32 * 128], BF16)
    w3t0, o = carve(o, [128, 44 * 128], BF16)
    w3t1, o = carve(o, [128, 44 * 128], BF16)
    silu_t, o = carve(o, [128, 512], F32)
    dstage0, o = carve(o, [128, 512], F32)
    dstage1, o = carve(o, [128, 512], F32)
    assert o <= ROPE_OFF, o

    o = DQ_END
    res2, o = carve_list(o, 32, [128, 512], F32)
    p_in0, o = carve(o, [128, 512], F32)
    p_in1, o = carve(o, [128, 512], F32)
    p_in2, o = carve(o, [128, 512], F32)
    l2_sq, o = carve(o, [128, 512], F32)
    l2_mean, o = carve(o, [128, 512], F32)
    l2_var, o = carve(o, [128, 512], F32)
    l2_rstd, o = carve(o, [128, 512], F32)
    l2_std, o = carve(o, [128, 512], F32)
    l2_b, o = carve(o, [128, 512], F32)
    onorm, o = carve(o, [128, 512], F32)
    ostage, o = carve(o, [128, 4 * 128], F32)
    assert o <= ROPE_OFF, o

    ps_a = nc.alloc_psum_tensor("ps_a", [128, 512], F32)
    ps_b = nc.alloc_psum_tensor("ps_b", [128, 512], F32)
    psc = nc.alloc_psum_tensor("psc", [128, 640], F32)
    pt = nc.alloc_psum_tensor("pt", [128, 1024], BF16)
    ps_e = nc.alloc_psum_tensor("ps_e", [128, 512], F32)
    st_a = nc.alloc_psum_tensor("st_a", [128, 512], F32)
    st_b = nc.alloc_psum_tensor("st_b", [128, 512], F32)

    TB_67 = (st_a.bitcast(BF16)[:, 0:128], st_b.bitcast(BF16)[:, 0:128])
    TB_23 = (psc.bitcast(BF16)[:, 0:128], psc.bitcast(BF16)[:, 1024:1152])
    VT_42 = (pt[:, 640:768], psc.bitcast(BF16)[:, 0:128])

    S_wsl = Stream(kb, "wsl", "sync")
    S_w = Stream(kb, "w", "sync")
    S_x = Stream(kb, "x", "gpsimd")
    S_misc = Stream(kb, "misc", "sync")
    S_dqst = Stream(kb, "dqst", "scalar")
    S_aw = Stream(kb, "aw", "scalar")
    S_ar = Stream(kb, "ar", "sync")
    S_xr = Stream(kb, "xr", "sync")
    S_xmw = Stream(kb, "xmw", "sync")
    S_xshw = Stream(kb, "xshw", "gpsimd")
    S_gw = Stream(kb, "gw", "sync")
    S_g3 = Stream(kb, "g3", "scalar")
    S_xmr = Stream(kb, "xmr", "sync")
    S_pw = Stream(kb, "pw", "gpsimd")
    S_pr = Stream(kb, "pr", "gpsimd")
    S_xfr = Stream(kb, "xfr", "sync")
    S_out = Stream(kb, "out", "scalar")

    def pool_prolog(g):
        cid = nc.partition_id([ET.Pool])
        kb.par["j"] = nc.snap(cid % 2, [ET.Pool], min_val=0, max_val=1)

    em("gpsimd", pool_prolog)

    t_id = S_misc.issue(lambda e: e.dma_start(out=ident_bf[:, :], in_=idbf[:, :]))
    S_misc.issue(lambda e: e.dma_start(out=ident_f[:, :], in_=idf[:, :]))
    t_ones = S_misc.issue(lambda e: e.dma_start(out=ones_f[:, :], in_=onesf[:, :]))
    S_misc.issue(lambda e: e.dma_start(out=cosq_sb, in_=cosq[:, :]))
    S_misc.issue(lambda e: e.dma_start(out=sinq_sb, in_=sinq[:, :]))
    S_misc.issue(lambda e: e.dma_start(out=cosk_sb, in_=cosk[:, :]))
    t_rope = S_misc.issue(lambda e: e.dma_start(out=sink_sb, in_=sink[:, :]))

    class DQ:
        def __init__(self):
            self.units = []
            for lc in range(NLC):
                self.units.append((0, lc))
                self.units.append((1, lc))
            for dk in range(NDK):
                self.units.append((2, dk))
            self.ui = 0
            self.load_toks = {}
            self.load_gate = {}
            self.li = 0
            self.store_tok = {}
            self.ev = {}
            for _ in range(2):
                self._load_next()

        def nsubs(self, u):
            return 4 if self.units[u][0] < 2 else 2

        def _sub_of(self, li):
            u = 0
            while True:
                ns = self.nsubs(u)
                if li < ns:
                    return u, li
                li -= ns
                u += 1

        def total_subs(self):
            return sum(self.nsubs(u) for u in range(len(self.units)))

        def _load_next(self):
            if self.li >= self.total_subs():
                return
            u, su = self._sub_of(self.li)
            w, lc = self.units[u]
            buf = dq_in0 if (self.li % 2 == 0) else dq_in1
            gates = [self.load_gate.get(self.li - 2)]
            if w < 2:
                src = (w1q if w == 0 else w2q)[lc * 128:(lc + 1) * 128, su * 1024:(su + 1) * 1024]
            else:
                src = w3q[lc * 128:(lc + 1) * 128, su * 1024:(su + 1) * 1024]
            tok = S_wsl.issue(lambda e, s=src, b=buf: e.dma_start(out=b[:, :], in_=s), gates=gates)
            self.load_toks[self.li] = tok
            self.li += 1

        def pump(self, n, tbanks):
            if DQ_DISABLE:
                return
            done = 0
            while done < n and self.ui < len(self.units):
                u = self.ui
                w, lc = self.units[u]
                ns = self.nsubs(u)
                base_li = sum(self.nsubs(x) for x in range(u))
                for su in range(ns):
                    li = base_li + su
                    while self.li <= li:
                        self._load_next()
                    self._emit_sub(u, su, li)
                    self._load_next()
                self._emit_T(u, tbanks)
                self.ui += 1
                done += 1

        def _emit_sub(self, u, su, li):
            w, lc = self.units[u]
            FD = 1024
            NB = 16
            buf = dq_in0 if (li % 2 == 0) else dq_in1
            wait("vector", self.load_toks[li], getattr(self, "last_T", None))

            def dve_fence():
                dmk = mark("dve")
                wait("gpsimd", dmk)
                em("gpsimd", lambda g: g.nop())
                pmk = mark("pool")
                wait("vector", pmk)

            def blk1a(v, buf=buf, NB=NB, FD=FD):
                w3v = buf[:, 0:FD].rearrange("p (nb bl) -> p nb bl", bl=64)
                return v.reduce_max(dq_am[:, 0:NB].unsqueeze(2), w3v, axis=AX.X,
                                    apply_absolute_value=True)

            def blk1b(v, FD=FD):
                v.memset(dq_a0[:, 0:FD], CODE0_BF)
                return v.tensor_scalar_max(dq_am2[:, 0:64], dq_am[:, 0:64], 1e-38)

            def blk2(v):
                v.memset(dq_ind[:, 0:64], 0.0)
                return v.reciprocal(dq_rcp[:, 0:64], dq_am2[:, 0:64])

            def blk3(v, buf=buf, FD=FD, NB=NB, su=su):
                w3v = buf[:, 0:FD].rearrange("p (nb bl) -> p nb bl", bl=64)
                rb = dq_rcp[:, 0:NB].unsqueeze(2).broadcast_to([128, NB, 64])
                v.tensor_tensor(dq_v[:, 0:FD].rearrange("p (nb bl) -> p nb bl", bl=64), w3v, rb, ALU.mult)
                bufs = [dq_a0, dq_a1]
                c = 0
                for bnd in range(0, 15):
                    v.tensor_scalar(dq_ind[:, 0:FD], dq_v[:, 0:FD], float(NF4_BOUNDS[bnd]),
                                    float(DELTAS_BF[bnd]), ALU.is_gt, ALU.mult)
                    v.tensor_tensor(bufs[1 - c][:, 0:FD], dq_ind[:, 0:FD], bufs[c][:, 0:FD], ALU.add)
                    c = 1 - c
                ab = dq_am2[:, 0:NB].unsqueeze(2).broadcast_to([128, NB, 64])
                return v.tensor_tensor(
                    dq_bf[:, su * FD:(su + 1) * FD].rearrange("p (nb bl) -> p nb bl", bl=64),
                    bufs[c][:, 0:FD].rearrange("p (nb bl) -> p nb bl", bl=64), ab, ALU.mult)

            em("vector", blk1a)
            dve_fence()
            em("vector", blk1b)
            dve_fence()
            em("vector", blk2)
            dve_fence()
            em("vector", blk3)
            self.load_gate[li] = self.sub_done = mark("dve")
            if DQ_TAP == (u, su):
                S_misc.issue(lambda e, b=buf: e.dma_start(out=tap_in[:, :], in_=b[:, :]),
                             gates=[self.sub_done])
                S_misc.issue(lambda e: e.dma_start(out=tap_v[:, :], in_=dq_v[:, :]))
                S_misc.issue(lambda e: e.dma_start(out=tap_am[:, :], in_=dq_am2[:, :]))
                S_misc.issue(lambda e: e.dma_start(out=tap_bf[:, :], in_=dq_bf[:, :]))

        def _emit_T(self, u, tbanks):
            w, lc = self.units[u]
            wait("tensor", self.sub_done)
            ntile = NDK if w < 2 else NLC
            stok = None
            for t in range(ntile):
                sl = t % 2
                bank = tbanks[sl]
                wait("tensor", self.ev.get((id(tbanks[0]), sl)))
                em("tensor", lambda te, b=bank, s=dq_bf[:, t * 128:(t + 1) * 128]:
                   te.transpose(b, s, ident_bf[:, :]))
                pm = mark("pe")
                wait("scalar", pm)
                stg = dq_ts[:, sl * 128:(sl + 1) * 128]
                wait("scalar", self.stg_rd.get(sl) if hasattr(self, "stg_rd") else None)
                em("scalar", lambda sc, b=bank, d=stg: sc.activation(out=d, in_=b, func=AF.Copy))
                av = mark("act")
                self.ev[(id(tbanks[0]), sl)] = av
                dst = slab12[w, lc, t] if w < 2 else slab3[lc, t]
                stok = S_dqst.issue(lambda e, d=dst, s=stg: e.dma_start(out=d, in_=s), gates=[av])
                if not hasattr(self, "stg_rd"):
                    self.stg_rd = {}
                self.stg_rd[sl] = stok
                self.last_T_pe = pm
            self.last_T = self.last_T_pe
            self.store_tok[u] = stok

    dq = DQ()

    def u_idx(w, lc):
        return dq.units.index((w, lc) if w < 2 else (2, lc))

    def emit_ag(region_in, region_out, after_units):
        for u in after_units:
            wait("gpsimd", dq.store_tok[u])
        sem = nc.alloc_semaphore(f"cc{kb.cc_n}")
        kb.cc_sems.append(sem)
        em("gpsimd", lambda g, i=region_in, o_=region_out, s=sem: g.collective_compute(
            "AllGather", ALU.bypass, replica_groups=GROUPS4, ins=[i], outs=[o_]).then_inc(s, 1))
        kb.cc_n += 1
        return ("cc", kb.cc_n)

    def emit_pair_barrier():
        sem = nc.alloc_semaphore(f"cc{kb.cc_n}")
        kb.cc_sems.append(sem)
        em("gpsimd", lambda g, s=sem: g.collective_compute(
            "AllReduce", ALU.add, replica_groups=PAIRS,
            ins=[ccin.ap().opt()], outs=[ccout.ap().opt()]).then_inc(s, 1))
        kb.cc_n += 1
        return ("cc", kb.cc_n)

    def flat12(w, l0, l1):
        return slab12[w, l0:l1].rearrange("a b p f -> (a b p) f")

    def gflat12(w, part):
        return ghw[(w, part)].ap().rearrange("r a b p f -> (r a b p) f")


    def wload(stream, blk, f0, f1, bfdst, wg, pre_cast_gate=None):
        marks = [wg[0], wg[1]]
        for qd in range(4):
            fb = (f0, f1)[qd % 2]
            tok = stream.issue(lambda e, s=blk, qd=qd, fb=fb: e.dma_start(
                out=fb[:, :].rearrange("p (a f) -> p a f", f=128),
                in_=s[qd * 1024:(qd + 1) * 1024, :].rearrange(
                    "(a p) f -> p a f", p=128)), gates=[marks[qd % 2]])
            if qd == 0 and pre_cast_gate is not None:
                wait("scalar", pre_cast_gate)
            wait("scalar", tok)
            em("scalar", lambda sc, fb=fb, qd=qd: sc.activation(
                out=bfdst[:, qd * 1024:(qd + 1) * 1024], in_=fb[:, :], func=AF.Copy))
            marks[qd % 2] = mark("act")
        return marks

    wait("tensor", t_rope)
    wait("vector", t_rope)

    em("vector", lambda v: v.memset(dq_am[:, :], 1.0))
    em("vector", lambda v: v.memset(dq_am2[:, :], 1.0))
    em("vector", lambda v: v.memset(ssum[:, :], 1.0))

    x_toks = []
    for c in range(32):
        x_toks.append(S_x.issue(lambda e, c=c: e.dma_start(
            out=xbf[c][:, :], in_=xT[c * 128:(c + 1) * 128, :])))

    wgate = [None, None]
    accum_guard = {0: None, 1: None}
    vt_guard = {0: None, 1: None}
    kv_pm = [None]
    for dc in range(min(2 * NKV, P1_CHUNKS)):
        is_v = dc >= NKV
        kvc = dc - NKV if is_v else dc
        wt = Wv if is_v else Wk
        wgate = wload(S_w, wt[kvc], wf0, wf1, wbf, wgate,
                      pre_cast_gate=(kv_pm[0] if dc > 0 else None))
        ca, cb = wgate
        for sl in range(3):
            bank = [ps_a, ps_b][sl % 2]
            wait("tensor", accum_guard[sl % 2])
            if sl == 0:
                wait("tensor", ca, cb)
                if dc == 0:
                    wait("tensor", x_toks[-1])
            for dnc in range(32):
                em("tensor", lambda te, l=wbf[:, dnc * 128:(dnc + 1) * 128],
                   r=xbf[dnc][:, sl * 512:(sl + 1) * 512], b=bank, st=(dnc == 0), sp=(dnc == 31):
                   te.matmul(b[:, :], l, r, start=st, stop=sp))
            pm = mark("pe")
            kv_pm[0] = pm
            if not is_v:
                wait("vector", pm)
                ck = cosk_sb[:, sl * 512:(sl + 1) * 512]
                sk0 = sink_sb[0:64, sl * 512:(sl + 1) * 512]
                sk1 = sink_sb[64:128, sl * 512:(sl + 1) * 512]
                em("vector", lambda v, b=bank, c=ck: v.tensor_tensor(rt1[:, :], b[:, :], c, ALU.mult))
                em("vector", lambda v, b=bank, s=sk0: v.tensor_tensor(rt2[0:64, :], b[64:128, :], s, ALU.mult))
                em("vector", lambda v, b=bank, s=sk1: v.tensor_tensor(rt2[64:128, :], b[0:64, :], s, ALU.mult))
                em("vector", lambda v, d=Kfm[kvc][:, sl * 512:(sl + 1) * 512]:
                   v.tensor_tensor(d, rt1[:, :], rt2[:, :], ALU.add))
                accum_guard[sl % 2] = mark("dve")
            else:
                wait("scalar", pm)
                em("scalar", lambda sc, b=bank, s=sl: sc.activation(
                    out=vfm[:, s * 512:(s + 1) * 512], in_=b[:, :], func=AF.Copy))
                accum_guard[sl % 2] = mark("act")
        if is_v:
            vev = curc("act")
            for tt in range(12):
                sl = tt % 2
                bank = VT_42[sl]
                wait("tensor", vt_guard[sl])
                if tt == 0:
                    wait("tensor", vev)
                em("tensor", lambda te, b=bank, s=vfm[:, tt * 128:(tt + 1) * 128]:
                   te.transpose(b, s, ident_bf[:, :]))
                pm2 = mark("pe")
                wait("scalar", pm2)
                em("scalar", lambda sc, b=bank, d=Vtm[tt][:, kvc * 128:(kvc + 1) * 128]:
                   sc.activation(out=d, in_=b, func=AF.Copy))
                vt_guard[sl] = mark("act")
        if ATTN_TAP and dc == 0:
            S_misc.issue(lambda e: e.dma_start(out=tap_k[:, :], in_=Kfm[0][:, :]),
                         gates=[curc("dve")])
        if ATTN_TAP and dc == NKV:
            S_misc.issue(lambda e: e.dma_start(out=tap_vt[:, :], in_=Vtm[5][:, :]),
                         gates=[curc("act")])
        dq.pump(1, TB_67)

    kv_pe = curc("pe")
    kv_act = curc("act")
    dq.pump(2, TB_67)

    if upto <= 1:
        dq.pump(1000, TB_67)
        if not DQ_DISABLE:
            wait("scalar", dq.store_tok[len(dq.units) - 1])
        elif P1_CHUNKS > 0:
            wait("scalar", curc("act"))
        return _materialize(kb, nc)

    assert dq.ui >= 12
    ag12a = ag12a2 = ag12b = ag12b2 = ("cc", 0)
    if upto >= 4:
        ag12a = emit_ag(flat12(0, 0, 6), gflat12(0, 0), [u_idx(0, lc) for lc in range(6)])
        ag12a2 = emit_ag(flat12(1, 0, 6), gflat12(1, 0), [u_idx(1, lc) for lc in range(6)])

    qgate = [None, None]
    msk_last_use = {0: kv_pe, 1: kv_pe}
    msk_tok = {}
    rope_guard = {0: None, 1: None}
    stage_rd = {0: None, 1: None}
    attn_w_tok = {}
    ptg = [None]
    pvg = [None]
    prev_ae = [None]
    prev_pmt = [None]
    qproj_pe = [kv_pe]

    for g in range(2):
        xg_toks = []
        for c in range(32):
            xg_toks.append(S_x.issue(lambda e, c=c, g=g: e.dma_start(
                out=xg[c][:, :], in_=xT[c * 128:(c + 1) * 128, 512 + g * 512:1024 + g * 512]),
                gates=[qproj_pe[0]]))
        qpm = {}
        for hc in range(32):
            if g == 0 and hc == 0:
                qgate = [kv_pe, kv_pe]
            wqb = [wqbf, wqbf2][hc % 2]
            qgate = wload(S_w, Wq[hc], wqf0, wqf1, wqb, qgate,
                          pre_cast_gate=qpm.get(hc - 2))
            ca, cb = qgate
            bank = [ps_a, ps_b][hc % 2]
            wait("tensor", rope_guard[hc % 2], ca, cb)
            if hc == 0:
                wait("tensor", xg_toks[-1])
            for dnc in range(32):
                em("tensor", lambda te, l=wqb[:, dnc * 128:(dnc + 1) * 128], r=xg[dnc][:, :],
                   b=bank, st=(dnc == 0), sp=(dnc == 31): te.matmul(b[:, :], l, r, start=st, stop=sp))
            pm = mark("pe")
            qpm[hc] = pm
            wait("vector", pm)
            cqs = cosq_sb[:, g * 512:(g + 1) * 512]
            sq0 = sinq_sb[0:64, g * 512:(g + 1) * 512]
            sq1 = sinq_sb[64:128, g * 512:(g + 1) * 512]
            em("vector", lambda v, b=bank, c=cqs: v.tensor_tensor(q_rt1[:, :], b[:, :], c, ALU.mult))
            em("vector", lambda v, b=bank, s=sq0: v.tensor_tensor(q_rt2[0:64, :], b[64:128, :], s, ALU.mult))
            em("vector", lambda v, b=bank, s=sq1: v.tensor_tensor(q_rt2[64:128, :], b[0:64, :], s, ALU.mult))
            em("vector", lambda v, d=qbuf[hc][:, :]: v.tensor_tensor(d, q_rt1[:, :], q_rt2[:, :], ALU.add))
            if ATTN_TAP and g == 1 and hc == 11:
                em("vector", lambda v, b=bank: v.tensor_copy(sm_f32[:, 0:512], b[:, :]))
            rope_guard[hc % 2] = mark("dve")
            if ATTN_TAP and g == 1 and hc == 11:
                S_misc.issue(lambda e: e.dma_start(out=tap_q[:, :], in_=qbuf[11][:, :]),
                             gates=[rope_guard[hc % 2]])
                S_misc.issue(lambda e: e.dma_start(out=tap_xg[:, :], in_=xg[5][:, :]))
                S_misc.issue(lambda e: e.dma_start(out=tap_wq[:, :], in_=wqbf[:, :]))
                S_misc.issue(lambda e: e.dma_start(out=tap_ps[:, :], in_=sm_f32[:, 0:512]))
            if hc % 8 == 7:
                dq.pump(1, TB_67)
        qproj_pe[0] = curc("pe")
        rope_all = curc("dve")
        wait("tensor", rope_all)
        for qt in range(4 * g, 4 * g + 4):
            qoff = (qt - 4 * g) * 128
            msl = qt % 2
            msk_tok[msl] = S_misc.issue(lambda e, qt=qt, msl=msl: e.dma_start(
                out=mask_sb[:, msl * 640:(msl + 1) * 640], in_=masks[qt]),
                gates=[msk_last_use[msl]])
            sc_cons = None
            for h in range(NH):
                kv = h // GQ
                wait("tensor", sc_cons)
                em("tensor", lambda te, l=qbuf[h][:, qoff:qoff + 128], r=Kfm[kv][:, qt * 128:qt * 128 + 640]:
                   te.matmul(psc[:, 0:512], l, r[:, 0:512], start=True, stop=True))
                em("tensor", lambda te, l=qbuf[h][:, qoff:qoff + 128], r=Kfm[kv][:, qt * 128:qt * 128 + 640]:
                   te.matmul(psc[:, 512:640], l, r[:, 512:640], start=True, stop=True))
                pm = mark("pe")
                wait("vector", pm, prev_ae[0])
                if h == 0:
                    wait("vector", msk_tok[msl])
                mk = mask_sb[:, msl * 640:(msl + 1) * 640]
                em("vector", lambda v, m=mk: v.tensor_tensor(sm_f32[:, :], psc[:, :], m, ALU.add))
                em("vector", lambda v: v.reduce_max(negmax[:, 0:1], sm_f32[:, :], axis=AX.X, negate=True))
                sc_cons = dm = mark("dve")
                if h == NH - 1:
                    msk_last_use[msl] = dm
                wait("scalar", dm)
                em("scalar", lambda sc: sc.activation(out=pm_bf[:, :], in_=sm_f32[:, :], func=AF.Exp,
                                                      bias=negmax[:, 0:1], scale=1.0,
                                                      accum_out=ssum[:, 0:1]))
                ae = mark("act")
                prev_ae[0] = ae
                wait("vector", ae)
                em("vector", lambda v: v.reciprocal(srecip[:, 0:64], ssum[:, 0:64]))
                dn = mark("dve")
                wait("scalar", dn, prev_pmt[0])
                em("scalar", lambda sc: sc.activation(out=exp_bf[:, :], in_=pm_bf[:, :], func=AF.Copy,
                                                      scale=srecip[:, 0:1]))
                dn = mark("act")
                if ATTN_TAP and qt == 4 and h == 0:
                    tt1 = S_misc.issue(lambda e: e.dma_start(out=tap_p[:, :], in_=exp_bf[:, :]), gates=[dn])
                    tt2 = S_misc.issue(lambda e: e.dma_start(out=tap_sm[:, :], in_=sm_f32[:, :]))
                    pass
                wait("tensor", dn, ptg[0])
                for c5 in range(5):
                    em("tensor", lambda te, c=c5: te.transpose(
                        pt[:, c * 128:(c + 1) * 128], exp_bf[:, c * 128:(c + 1) * 128], ident_bf[:, :]))
                pmt = mark("pe")
                prev_pmt[0] = pmt
                wait("scalar", pmt)
                em("scalar", lambda sc: sc.activation(out=pt_sb[:, :], in_=pt[:, 0:640], func=AF.Copy))
                ptg[0] = mark("act")
                wait("tensor", ptg[0], pvg[0])
                for c5 in range(5):
                    em("tensor", lambda te, c=c5, kv=kv, qt=qt: te.matmul(
                        ps_e[:, 0:128], Vtm[qt + c][:, kv * 128:(kv + 1) * 128],
                        pt_sb[:, c * 128:(c + 1) * 128], start=(c == 0), stop=(c == 4)))
                pvm = mark("pe")
                wait("scalar", pvm)
                stg = [astage0, astage1][qt % 2]
                if h == 0:
                    wait("scalar", stage_rd[qt % 2])
                em("scalar", lambda sc, d=stg, h=h: sc.activation(
                    out=d[:, h * 128:(h + 1) * 128], in_=ps_e[:, 0:128], func=AF.Copy))
                pvg[0] = mark("act")
                if h % 8 == 7:
                    dq.pump(1, TB_67)
            last_tok = None
            for hcc in range(32):
                stg = [astage0, astage1][qt % 2]
                last_tok = S_aw.issue(lambda e, d=adr[qt, hcc], s=stg[:, hcc * 128:(hcc + 1) * 128]:
                                      e.dma_start(out=d, in_=s), gates=[pvg[0] if hcc == 0 else None])
            stage_rd[qt % 2] = last_tok
            attn_w_tok[qt] = last_tok
        if g == 0 and upto >= 4:
            assert dq.ui >= 22, dq.ui
            ag12b = emit_ag(flat12(0, 6, NLC), gflat12(0, 1), [u_idx(0, lc) for lc in range(6, NLC)])
            ag12b2 = emit_ag(flat12(1, 6, NLC), gflat12(1, 1), [u_idx(1, lc) for lc in range(6, NLC)])

    p2_pe = curc("pe")
    p2_act = curc("act")
    p2_dve = curc("dve")
    dq.pump(3, TB_67)
    if upto <= 2:
        dq.pump(1000, TB_67)
        wait("scalar", dq.store_tok[len(dq.units) - 1], stage_rd[0], stage_rd[1])
        return _materialize(kb, nc)

    xsh_w_last = None
    wog = [None, None]
    prev_half_pe = [p2_pe]
    stats_rd = [None]
    for half in range(2):
        a_last = None
        for c in range(32):
            for tt in range(4):
                a_last = S_ar.issue(lambda e, c=c, tt=tt, half=half: e.dma_start(
                    out=ah[c][:, tt * 128:(tt + 1) * 128], in_=adr[4 * half + tt, c]),
                    gates=[attn_w_tok[4 * half + tt] if c == 0 else None,
                           p2_pe if (c == 0 and tt == 0 and half == 0) else None,
                           prev_half_pe[0] if (c == 0 and tt == 0 and half == 1) else None])
        ev_guard = {0: None, 1: None}
        stm = None
        wopm = {}
        prev_stm = [None]
        for dc in range(32):
            if half == 0 and dc == 0:
                wog = [p2_pe, p2_pe]
                wait("scalar", stage_rd[0], stage_rd[1])
                wait("vector", stage_rd[0], stage_rd[1], p2_act)
            wob = [wobf, wobf2][dc % 2]
            wog = wload(S_w, Wo[dc], wof0, wof1, wob, wog,
                        pre_cast_gate=(wopm.get(dc - 2) if dc >= 2 else
                                       (p2_pe if half == 0 else prev_half_pe[0])))
            ca, cb = wog
            xrb = [xr0, xr1][dc % 2]
            xtok = S_xr.issue(lambda e, dc=dc, half=half, d=xrb: e.dma_start(
                out=d[:, :], in_=xT[dc * 128:(dc + 1) * 128, 512 + half * 512:1024 + half * 512]),
                gates=[ev_guard[dc % 2]])
            bank = [ps_a, ps_b][dc % 2]
            wait("tensor", ev_guard[dc % 2], ca, cb)
            if dc == 0:
                wait("tensor", a_last)
            for dnc in range(32):
                em("tensor", lambda te, l=wob[:, dnc * 128:(dnc + 1) * 128], r=ah[dnc][:, :],
                   b=bank, st=(dnc == 0), sp=(dnc == 31): te.matmul(b[:, :], l, r, start=st, stop=sp))
            pm = mark("pe")
            wopm[dc] = pm
            wait("vector", xtok, pm)
            if LN1_TAP and half == 0 and dc == 0:
                em("vector", lambda v, b=bank: v.tensor_copy(xmid_c[:, :], b[:, :]))
                tpw = mark("dve")
                S_misc.issue(lambda e: e.dma_start(out=tap_in[:, 0:512], in_=xmid_c[:, :]), gates=[tpw])
            if LN1_TAP and half == 0 and dc == 0:
                em("vector", lambda v, xx=xrb: v.tensor_copy(ln_var[:, :], xx[:, :]))
                tpx = mark("dve")
                S_misc.issue(lambda e: e.dma_start(out=tap_v[:, 0:512], in_=ln_var[:, :]), gates=[tpx])
            em("vector", lambda v, b=bank, xx=xrb, d=res1[dc]: v.tensor_tensor(
                d[:, :], b[:, :], xx[:, :], ALU.add))
            rv = mark("dve")
            ev_guard[dc % 2] = rv
            if LN1_TAP and half == 0 and dc == 0:
                tearly = S_misc.issue(lambda e: e.dma_start(out=tap_bf[:, 1024:2048].bitcast(F32)[:, 0:512]
                                                            if False else tap_in[:, 512:1024], in_=res1[0][:, :]),
                                      gates=[rv])
                wait("vector", tearly)
                wait("tensor", tearly)
                wait("scalar", tearly)
            wait("scalar", rv, prev_stm[0])
            em("scalar", lambda sc, s=res1[dc]: sc.activation(out=sqt[:, :], in_=s[:, :], func=AF.Square))
            sqv = mark("act")
            wait("tensor", rv, sqv)
            if dc == 0 and half == 1:
                wait("tensor", stats_rd[0])
            em("tensor", lambda te, s=res1[dc], st=(dc == 0), sp=(dc == 31):
               te.matmul(st_a[:, :], ones_f[:, :], s[:, :], start=st, stop=sp))
            em("tensor", lambda te, st=(dc == 0), sp=(dc == 31):
               te.matmul(st_b[:, :], ones_f[:, :], sqt[:, :], start=st, stop=sp))
            stm = mark("pe")
            prev_stm[0] = stm
        wait("vector", stm)
        em("vector", lambda v: v.tensor_scalar(ln_mean[:, :], st_a[:, :], 1.0 / H, None, ALU.mult))
        em("vector", lambda v: v.tensor_scalar(ln_var[:, :], st_b[:, :], 1.0 / H, None, ALU.mult))
        stats_rd[0] = mark("dve")
        em("vector", lambda v: v.tensor_tensor(ln_b[:, :], ln_mean[:, :], ln_mean[:, :], ALU.mult))
        em("vector", lambda v: v.tensor_tensor(ln_var[:, :], ln_var[:, :], ln_b[:, :], ALU.subtract))
        sdone = mark("dve")
        em("vector", lambda v: v.tensor_scalar_add(ln_var[:, :], ln_var[:, :], EPS))
        sdone = mark("dve")
        wait("scalar", sdone)
        em("scalar", lambda sc: sc.activation(out=ln_std[:, :], in_=ln_var[:, :], func=AF.Sqrt))
        s2 = mark("act")
        wait("vector", s2)
        em("vector", lambda v: v.reciprocal(ln_rstd[:, :], ln_std[:, :]))
        em("vector", lambda v: v.scalar_tensor_tensor(ln_b[:, :], ln_mean[:, :], -1.0,
                                                      ln_rstd[:, :], ALU.mult, ALU.mult))
        mark("dve")
        if LN1_TAP and half == 0:
            t1 = S_misc.issue(lambda e: e.dma_start(out=tap_ps[:, :], in_=res1[0][:, :]),
                              gates=[curc("dve")])
            t2 = S_misc.issue(lambda e: e.dma_start(out=tap_q[:, :].bitcast(F32)[:, 0:256],
                                                    in_=ln_mean[:, 0:256]))
            t3 = S_misc.issue(lambda e: e.dma_start(out=tap_xg[:, :].bitcast(F32)[:, 0:256],
                                                    in_=ln_rstd[:, 0:256]))
            wait("vector", t3)
            wait("scalar", t3)
            wait("tensor", t3)
        xm_guard = None
        for dc in range(32):
            if xm_guard is not None:
                wait("vector", *xm_guard)
            em("vector", lambda v, s=res1[dc]: v.tensor_tensor(xmid_c[:, :], s[:, :], ln_rstd[:, :], ALU.mult))
            em("vector", lambda v: v.tensor_tensor(xmid_c[:, :], xmid_c[:, :], ln_b[:, :], ALU.add))
            xv = mark("dve")
            tok1 = S_xmw.issue(lambda e, dc=dc, half=half: e.dma_start(
                out=xmidf[dc, :, half * 512:(half + 1) * 512], in_=xmid_c[:, :]), gates=[xv])
            tok2 = S_xshw.issue(lambda e, dc=dc, half=half: e.dma_start(
                out=xstage[dc, :, half * 512:(half + 1) * 512], in_=xmid_c[:, :]), gates=[xv])
            xm_guard = (tok1, tok2)
            xsh_w_last = tok2
        prev_half_pe[0] = curc("pe")
        dq.pump(2, TB_23)

    p3_pe = curc("pe")
    p3_dve = curc("dve")
    p3_act = curc("act")
    xmw_last = xm_guard[0]

    if upto <= 3:
        dq.pump(1000, TB_23)
        wait("scalar", dq.store_tok[len(dq.units) - 1])
        wait("sync", xm_guard[0])
        wait("gpsimd", xsh_w_last)
        return _materialize(kb, nc)

    xcp = S_xshw.issue(lambda e: e.dma_start(out=xsh[kb.par["j"]], in_=xstage.ap()),
                       gates=[xsh_w_last])
    wait("gpsimd", xcp)
    bar_xmid = emit_pair_barrier()
    dq.pump(100, TB_23)
    assert dq.ui == len(dq.units)
    ag3a = emit_ag(slab3[0:16].rearrange("a b p f -> (a b p) f"),
                   gh3a.ap().rearrange("r a b p f -> (r a b p) f"),
                   [u_idx(2, dk) for dk in range(16)])
    ag3b = emit_ag(slab3[16:32].rearrange("a b p f -> (a b p) f"),
                   gh3b.ap().rearrange("r a b p f -> (r a b p) f"),
                   [u_idx(2, dk) for dk in range(16, 32)])

    if "tgh3a" in taps:
        tgh3a = dint("tgh3a", [4, 16, NLC, 128, 128], BF16)
        tgh3b = dint("tgh3b", [4, 16, NLC, 128, 128], BF16)
        wait("sync", ag3a, ag3b)
        tg3t = S_misc.issue(lambda e: e.dma_start(out=tgh3a.ap(), in_=gh3a.ap()))
        tg3t = S_misc.issue(lambda e: e.dma_start(out=tgh3b.ap(), in_=gh3b.ap()))
        wait("tensor", tg3t)

    def hc_list():
        out = []
        for lc in range(NLC):
            for gr in range(4):
                if gr == 3 and lc >= 10:
                    continue
                out.append((gr, lc))
        return out

    HCS = hc_list()
    assert len(HCS) == NHC

    pall_w_last = None
    prev_hg = [None]
    xm_rd_gate = [p3_pe, p3_dve, p3_act]
    w12_guard = {0: [p3_pe, p3_dve, p3_act], 1: [None]}
    hact_guard = {0: None, 1: None}
    dstage_rd = {0: None, 1: None}
    prev_down_pe = [None]
    ag12_gates = {True: ("sync_done", None)}
    first_gw = [True]
    for tg in range(4):
        slot = tg // 2
        hh = tg % 2
        wait("sync", bar_xmid)
        xm_toks = []
        for c in range(32):
            xm_toks.append(S_xmr.issue(lambda e, c=c, slot=slot, hh=hh: e.dma_start(
                out=xmg[c][:, :], in_=xsh[slot][c][:, hh * 512:(hh + 1) * 512]),
                gates=xm_rd_gate if c == 0 else ()))
        if FFN_TAP and tg == 0:
            tfx = S_misc.issue(lambda e: e.dma_start(out=tap_q[:, :], in_=xmg[0][:, :]),
                               gates=[xm_toks[-1]])
            wait("sync", ag12a2)
            S_misc.issue(lambda e: e.dma_start(out=tap_bf[:, 0:128].rearrange("p (a f) -> p a f", f=128),
                                               in_=ghw[(0, 0)][1, 0, 0:1].rearrange("a p f -> p a f")))
            S_misc.issue(lambda e: e.dma_start(out=tap_bf[:, 128:256].rearrange("p (a f) -> p a f", f=128),
                                               in_=ghw[(0, 0)][3, 2, 5:6].rearrange("a p f -> p a f")))
            wait("sync", ag3b)
            S_misc.issue(lambda e: e.dma_start(out=tap_bf[:, 256:384].rearrange("p (a f) -> p a f", f=128),
                                               in_=gh3a[1, 0, 0:1].rearrange("a p f -> p a f")))
            S_misc.issue(lambda e: e.dma_start(out=tap_bf[:, 384:512].rearrange("p (a f) -> p a f", f=128),
                                               in_=gh3b[3, 21 - 16, 2:3].rearrange("a p f -> p a f")))
            em("sync", lambda e, v=ag3a[1] if False else 0: e.nop())
            wait("tensor", tfx)
        for ci, (gr, lc) in enumerate(HCS):
            sl = ci % 2
            w1b = [w1t0, w1t1][sl]
            w2b = [w2t0, w2t1][sl]
            if first_gw[0] or (tg == 0 and lc >= 6 and (gr, lc) == (0, 6)):
                need = ag12a2 if first_gw[0] else ag12b2
                wait("sync", need)
                if tg == 0 and not first_gw[0]:
                    pass
                first_gw[0] = False
            gts = w12_guard[sl] if not isinstance(w12_guard[sl], list) else w12_guard[sl]
            t1 = S_gw.issue(lambda e, gr=gr, lc=lc, d=w1b: e.dma_start(
                out=d[:, :].rearrange("p (a f) -> p a f", f=128),
                in_=(ghw[(0, 0)][gr, lc] if lc < 6 else ghw[(0, 1)][gr, lc - 6]).rearrange("a p f -> p a f")),
                gates=gts if isinstance(gts, list) else [gts])
            t2 = S_gw.issue(lambda e, gr=gr, lc=lc, d=w2b: e.dma_start(
                out=d[:, :].rearrange("p (a f) -> p a f", f=128),
                in_=(ghw[(1, 0)][gr, lc] if lc < 6 else ghw[(1, 1)][gr, lc - 6]).rearrange("a p f -> p a f")))
            gbank = [ps_a, ps_b][sl]
            ubank = [ps_e[:, :], psc[:, 0:512]][sl]
            wait("tensor", hact_guard[sl], t2)
            if ci == 0:
                wait("tensor", xm_toks[-1])
                if tg == 0:
                    wait("tensor", p3_dve, p3_act)
            for dnc in range(32):
                em("tensor", lambda te, l=w1b[:, dnc * 128:(dnc + 1) * 128], r=xmg[dnc][:, :],
                   b=gbank, st=(dnc == 0), sp=(dnc == 31): te.matmul(b[:, :], l, r, start=st, stop=sp))
            gm = mark("pe")
            for dnc in range(32):
                em("tensor", lambda te, l=w2b[:, dnc * 128:(dnc + 1) * 128], r=xmg[dnc][:, :],
                   b=ubank, st=(dnc == 0), sp=(dnc == 31): te.matmul(b, l, r, start=st, stop=sp))
            um = mark("pe")
            wait("scalar", gm, prev_hg[0])
            em("scalar", lambda sc, b=gbank: sc.activation(out=silu_t[:, :], in_=b[:, :], func=AF.Silu))
            sv = mark("act")
            wait("vector", sv, um)
            if prev_down_pe[0] is not None and ci == 0:
                wait("vector", prev_down_pe[0])
            em("vector", lambda v, u=ubank, d=hact[ci]: v.tensor_tensor(d[:, :], silu_t[:, :], u, ALU.mult))
            hg = mark("dve")
            if FFN_TAP and tg == 0 and ci in (0, 5):
                S_misc.issue(lambda e, ci=ci: e.dma_start(
                    out=tap_vt[:, (0 if ci == 0 else 512):(512 if ci == 0 else 1024)].bitcast(BF16)[:, 0:512],
                    in_=hact[ci][:, :]), gates=[hg])
            hact_guard[sl] = hg
            w12_guard[sl] = hg
        hact_all = curc("dve")
        if FFN_TAP and tg == 0:
            htok = None
            for ci in range(NHC):
                htok = S_misc.issue(lambda e, ci=ci: e.dma_start(out=tap_hact[ci], in_=hact[ci][:, :]),
                                    gates=[hact_all if ci == 0 else None])
            wait("tensor", htok)
        w3_guard = {0: None, 1: None}
        wait("tensor", hact_all)
        for dc in range(32):
            sl = dc % 2
            w3b = [w3t0, w3t1][sl]
            if tg == 0 and dc == 0:
                wait("scalar", ag3a)
            if tg == 0 and dc == 16:
                wait("scalar", ag3b)
            toks = []
            for gr in range(4):
                toks.append(S_g3.issue(lambda e, gr=gr, dc=dc, d=w3b: e.dma_start(
                    out=d[:, (gr * NLC) * 128:((gr + 1) * NLC) * 128].rearrange("p (a f) -> p a f", f=128),
                    in_=(gh3a[gr, dc] if dc < 16 else gh3b[gr, dc - 16]).rearrange("a p f -> p a f")),
                    gates=[w3_guard[sl] if gr == 0 else None,
                           (p3_act if (tg == 0 and dc == 0 and gr == 0) else None)]))
            bank = [ps_a, ps_b][sl]
            wait("tensor", dstage_rd[sl], toks[-1])
            if FFN_TAP and tg == 0 and dc == 0:
                S_misc.issue(lambda e, d=w3b: e.dma_start(out=tap_bf[:, 0:4096],
                                                          in_=d[:, 0:4096]), gates=[toks[-1]])
                S_misc.issue(lambda e, d=w3b: e.dma_start(
                    out=tap_hact[0:32, :, 0:128].rearrange("a p f -> p a f"),
                    in_=d[:, 12 * 128:44 * 128].rearrange("p (a f) -> p a f", f=128)))
            first = True
            for ci, (gr, lc) in enumerate(HCS):
                em("tensor", lambda te, l=w3b[:, (gr * NLC + lc) * 128:(gr * NLC + lc + 1) * 128],
                   r=hact[ci][:, :], b=bank, st=first, sp=(ci == NHC - 1):
                   te.matmul(b[:, :], l, r, start=st, stop=sp))
                first = False
            dm = mark("pe")
            w3_guard[sl] = dm
            if FFN_TAP and tg == 0 and dc == 0:
                wait("vector", dm)
                em("vector", lambda v, b=bank: v.tensor_copy(silu_t[:, :], b[:, :]))
                tdp = mark("dve")
                S_misc.issue(lambda e: e.dma_start(out=tap_ps[:, :], in_=silu_t[:, :]), gates=[tdp])
                wait("scalar", tdp)
            wait("scalar", dm)
            stg = [dstage0, dstage1][sl]
            wait("scalar", dstage_rd[sl])
            em("scalar", lambda sc, b=bank, d=stg: sc.activation(out=d[:, :], in_=b[:, :], func=AF.Copy))
            av = mark("act")
            tok = S_pw.issue(lambda e, dc=dc, slot=slot, hh=hh, s=stg: e.dma_start(
                out=pstage[slot, dc, :, hh * 512:(hh + 1) * 512], in_=s[:, :]), gates=[av])
            pall_w_last = tok
            dstage_rd[sl] = tok
        prev_down_pe[0] = curc("pe")
        xm_rd_gate = [curc("pe")]

    if upto <= 4:
        wait("gpsimd", pall_w_last)
        return _materialize(kb, nc)
    pcp0 = S_pw.issue(lambda e: e.dma_start(out=pall[kb.par["j"], 0], in_=pstage[0]),
                      gates=[pall_w_last])
    pcp1 = S_pw.issue(lambda e: e.dma_start(out=pall[kb.par["j"], 1], in_=pstage[1]))
    wait("gpsimd", pcp1)
    bar_pall = emit_pair_barrier()
    p4_pe = curc("pe")
    p4_dve = curc("dve")
    p4_act = curc("act")

    out_last = None
    pg = [[p4_pe, p4_dve, p4_act]]
    wait("gpsimd", bar_pall)
    prcp0 = S_pr.issue(lambda e: e.dma_start(out=prstage[0], in_=pall[0, kb.par["j"]]))
    prcp1 = S_pr.issue(lambda e: e.dma_start(out=prstage[1], in_=pall[1, kb.par["j"]]))
    l2_stats_rd = [None]
    for half in range(2):
        stm2 = None
        prev_stm2 = [None]
        for dc in range(32):
            tok0 = S_pr.issue(lambda e, dc=dc, half=half: e.dma_start(
                out=p_in0[:, :], in_=prstage[0, dc, :, half * 512:(half + 1) * 512]),
                gates=pg[0] + ([prcp1] if (half == 0 and dc == 0) else []))
            tok1 = S_pr.issue(lambda e, dc=dc, half=half: e.dma_start(
                out=p_in1[:, :], in_=prstage[1, dc, :, half * 512:(half + 1) * 512]))
            tok2 = S_xfr.issue(lambda e, dc=dc, half=half: e.dma_start(
                out=p_in2[:, :], in_=xmidf[dc, :, half * 512:(half + 1) * 512]),
                gates=[xmw_last if (half == 0 and dc == 0) else None] + pg[0])
            wait("vector", tok0, tok1, tok2)
            em("vector", lambda v, d=res2[dc]: v.tensor_tensor(d[:, :], p_in0[:, :], p_in1[:, :], ALU.add))
            em("vector", lambda v, d=res2[dc]: v.tensor_tensor(d[:, :], d[:, :], p_in2[:, :], ALU.add))
            rv = mark("dve")
            pg[0] = [rv]
            wait("scalar", rv, prev_stm2[0])
            em("scalar", lambda sc, s=res2[dc]: sc.activation(out=l2_sq[:, :], in_=s[:, :], func=AF.Square))
            sv = mark("act")
            wait("tensor", rv, sv)
            if dc == 0 and half == 1:
                wait("tensor", l2_stats_rd[0])
            em("tensor", lambda te, s=res2[dc], st=(dc == 0), sp=(dc == 31):
               te.matmul(st_a[:, :], ones_f[:, :], s[:, :], start=st, stop=sp))
            em("tensor", lambda te, st=(dc == 0), sp=(dc == 31):
               te.matmul(st_b[:, :], ones_f[:, :], l2_sq[:, :], start=st, stop=sp))
            stm2 = mark("pe")
            prev_stm2[0] = stm2
        wait("vector", stm2)
        em("vector", lambda v: v.tensor_scalar(l2_mean[:, :], st_a[:, :], 1.0 / H, None, ALU.mult))
        em("vector", lambda v: v.tensor_scalar(l2_var[:, :], st_b[:, :], 1.0 / H, None, ALU.mult))
        l2_stats_rd[0] = mark("dve")
        em("vector", lambda v: v.tensor_tensor(l2_b[:, :], l2_mean[:, :], l2_mean[:, :], ALU.mult))
        em("vector", lambda v: v.tensor_tensor(l2_var[:, :], l2_var[:, :], l2_b[:, :], ALU.subtract))
        sdone = mark("dve")
        em("vector", lambda v: v.tensor_scalar_add(l2_var[:, :], l2_var[:, :], EPS))
        sdone = mark("dve")
        wait("scalar", sdone)
        em("scalar", lambda sc: sc.activation(out=l2_std[:, :], in_=l2_var[:, :], func=AF.Sqrt))
        s2m = mark("act")
        wait("vector", s2m)
        em("vector", lambda v: v.reciprocal(l2_rstd[:, :], l2_std[:, :]))
        em("vector", lambda v: v.scalar_tensor_tensor(l2_b[:, :], l2_mean[:, :], -1.0,
                                                      l2_rstd[:, :], ALU.mult, ALU.mult))
        mark("dve")
        tguard = {0: None, 1: None}
        nvg = None
        for dc in range(32):
            if nvg is not None:
                wait("vector", nvg)
            em("vector", lambda v, s=res2[dc]: v.tensor_tensor(onorm[:, :], s[:, :], l2_rstd[:, :], ALU.mult))
            em("vector", lambda v: v.tensor_tensor(onorm[:, :], onorm[:, :], l2_b[:, :], ALU.add))
            nv = mark("dve")
            wait("tensor", nv)
            if out_last is not None:
                wait("scalar", out_last)
            for tt in range(4):
                sl = tt % 2
                bap = [ps_e[:, 0:128], psc[:, 0:128]][sl]
                wait("tensor", tguard[sl])
                em("tensor", lambda te, b=bap, s=onorm[:, tt * 128:(tt + 1) * 128]:
                   te.transpose(b, s, ident_f[:, :]))
                tm = mark("pe")
                wait("scalar", tm)
                em("scalar", lambda sc, b=bap, d=ostage[:, tt * 128:(tt + 1) * 128]:
                   sc.activation(out=d, in_=b, func=AF.Copy))
                tguard[sl] = mark("act")
            ost = curc("act")
            nvg = curc("pe")
            for tt in range(4):
                out_last = S_out.issue(lambda e, dc=dc, half=half, tt=tt: e.dma_start(
                    out=y[half * 512 + tt * 128:half * 512 + (tt + 1) * 128, dc * 128:(dc + 1) * 128],
                    in_=ostage[:, tt * 128:(tt + 1) * 128]), gates=[ost if tt == 0 else None])
    wait("scalar", out_last)
    return _materialize(kb, nc)


def _materialize(kb, nc):
    with nc.Block() as block:
        @block.sync
        def _(e):
            for f in kb.ops["sync"]:
                f(e)

        @block.scalar
        def _(e):
            for f in kb.ops["scalar"]:
                f(e)

        @block.vector
        def _(e):
            for f in kb.ops["vector"]:
                f(e)

        @block.tensor
        def _(e):
            for f in kb.ops["tensor"]:
                f(e)

        @block.gpsimd
        def _(e):
            for f in kb.ops["gpsimd"]:
                f(e)

    return nc


def _host_inputs(x, Wq, Wk, Wv, Wo, w1, w2, w3):
    ins = []
    Wq_t = np.ascontiguousarray(Wq.reshape(H, 32, 128).transpose(1, 0, 2))
    Wk_t = np.ascontiguousarray(Wk.reshape(H, 8, 128).transpose(1, 0, 2))
    Wv_t = np.ascontiguousarray(Wv.reshape(H, 8, 128).transpose(1, 0, 2))
    Wo_t = np.ascontiguousarray(Wo.reshape(H, 32, 128).transpose(1, 0, 2))
    idbf = np.eye(128, dtype=bfnp)
    idf = np.eye(128, dtype=np.float32)
    onesf = np.ones((128, 128), dtype=np.float32)
    invf = (1.0 / (10000.0 ** (np.arange(64, dtype=np.float64) / 64.0)))
    for c in range(8):
        b, chunk = divmod(c, 4)
        q0 = chunk * T
        xt = np.zeros((KT, H), dtype=np.float32)
        lo = q0 - 512
        v0 = max(0, lo)
        xt[v0 - lo:, :] = x[b, v0:q0 + T, :]
        xT = np.ascontiguousarray(xt.T)
        posq = np.arange(q0, q0 + T, dtype=np.float64)
        posk = np.clip(np.arange(lo, q0 + T, dtype=np.float64), 0, None)
        angq = posq[None, :] * invf[:, None]
        angk = posk[None, :] * invf[:, None]
        sc = 1.0 / np.sqrt(HD)
        cosq_ = (np.concatenate([np.cos(angq), np.cos(angq)], 0) * sc).astype(bfnp)
        sinq_ = (np.concatenate([-np.sin(angq), np.sin(angq)], 0) * sc).astype(bfnp)
        cosk_ = np.concatenate([np.cos(angk), np.cos(angk)], 0).astype(bfnp)
        sink_ = np.concatenate([-np.sin(angk), np.sin(angk)], 0).astype(bfnp)
        mk = np.zeros((8, 128, 640), dtype=bfnp)
        qr = np.arange(128)[:, None]
        kr = np.arange(640)[None, :]
        band = (kr >= qr) & (kr <= qr + 512)
        for qt in range(8):
            kabs = q0 + qt * 128 - 512 + kr
            ok = band & (kabs >= 0)
            mk[qt] = np.where(ok, np.float32(0.0), np.float32(-1e30)).astype(bfnp)
        g = c // 2
        j = c % 2
        hbase = j * HHALF + QOFF[g]
        qlen = QL[g]
        w1q = np.zeros((1408, H), dtype=np.float32)
        w2q = np.zeros((1408, H), dtype=np.float32)
        w3q = np.zeros((H, 2048), dtype=np.float32)
        w1q[:qlen] = w1[hbase:hbase + qlen, :]
        w2q[:qlen] = w2[hbase:hbase + qlen, :]
        w3q[:, :qlen] = w3[:, hbase:hbase + qlen]
        ins.append({
            "xT": xT, "cosq": cosq_, "sinq": sinq_, "cosk": cosk_, "sink": sink_,
            "masks": mk, "idbf": idbf, "idf": idf, "onesf": onesf,
            "Wq": Wq_t, "Wk": Wk_t, "Wv": Wv_t, "Wo": Wo_t,
            "w1q": np.ascontiguousarray(w1q), "w2q": np.ascontiguousarray(w2q),
            "w3q": np.ascontiguousarray(w3q),
        })
    return ins


_NC_CACHE = {}


def kernel(L0_attn_hidden_states, L0_attn_Wq, L0_attn_Wk, L0_attn_Wv, L0_attn_Wo,
           L0_ffn_w1, L0_ffn_w2, L0_ffn_w3):
    x = np.asarray(L0_attn_hidden_states, dtype=np.float32)
    ins = _host_inputs(x, np.asarray(L0_attn_Wq, np.float32), np.asarray(L0_attn_Wk, np.float32),
                       np.asarray(L0_attn_Wv, np.float32), np.asarray(L0_attn_Wo, np.float32),
                       np.asarray(L0_ffn_w1, np.float32), np.asarray(L0_ffn_w2, np.float32),
                       np.asarray(L0_ffn_w3, np.float32))
    if "nc" not in _NC_CACHE:
        _NC_CACHE["nc"] = build()
    nc = _NC_CACHE["nc"]
    res = run_bass_kernel_spmd(nc, ins, core_ids=list(range(8)))
    out = np.empty((B, S, H), dtype=np.float32)
    for c in range(8):
        b, chunk = divmod(c, 4)
        out[b, chunk * T:(chunk + 1) * T, :] = res.results[c]["y"]
    return out


# version 04 (final)
# speedup vs baseline: 1.0102x; 1.0102x over previous
import numpy as np
import ml_dtypes

import concourse.bass as bass
import concourse.mybir as mybir
from concourse.bass_utils import run_bass_kernel_spmd

F32 = mybir.dt.float32
BF16 = mybir.dt.bfloat16
AF = mybir.ActivationFunctionType
ALU = mybir.AluOpType
AX = mybir.AxisListType
ET = mybir.EngineType
bfnp = ml_dtypes.bfloat16

B, S, H = 2, 4096, 4096
NH, NKV, HD, GQ = 32, 8, 128, 4
T = 1024
KT = 1536
DFF = 11008
HHALF = DFF // 2
QL = [1408, 1408, 1408, 1280]
QOFF = [0, 1408, 2816, 4224]
NLC = 11
NDK = 32
NHC = 43
EPS = 1e-5

NF4_CODE = np.array([-1.0, -0.6961928009986877, -0.5250730514526367, -0.39491748809814453,
                     -0.28444138169288635, -0.18477343022823334, -0.09105003625154495, 0.0,
                     0.07958029955625534, 0.16093020141124725, 0.24611230194568634,
                     0.33791524171829224, 0.44070982933044434, 0.5626170039176941,
                     0.7229568362236023, 1.0], dtype=np.float32)
NF4_BOUNDS = ((NF4_CODE[:-1] + NF4_CODE[1:]) * 0.5).astype(np.float32)


def _bf16_chain_consts():
    cb = NF4_CODE.astype(bfnp)
    deltas = []
    for b in range(15):
        d0 = bfnp(np.float32(cb[b + 1]) - np.float32(cb[b]))
        du = d0.view(np.uint16)
        pick = None
        for off in (0, 1, -1, 2, -2):
            d = np.uint16(du + off).view(bfnp)
            if bfnp(np.float32(cb[b]) + np.float32(d)).view(np.uint16) == cb[b + 1].view(np.uint16):
                pick = d
                break
        assert pick is not None
        deltas.append(float(np.float32(pick)))
    return float(np.float32(cb[0])), deltas


CODE0_BF, DELTAS_BF = _bf16_chain_consts()

DQ_DISABLE = False
P1_CHUNKS = 16
DQ_TAP = None
ATTN_TAP = False
LN1_TAP = False
FFN_TAP = False

ENG_OF = {"pe": "tensor", "dve": "vector", "act": "scalar", "pool": "gpsimd"}


class Kb:

    def __init__(self):
        nc = bass.Bass("TRN2", target_bir_lowering=False)
        self.nc = nc
        self.ops = {k: [] for k in ("sync", "scalar", "vector", "tensor", "gpsimd")}
        self.cnt = {k: [nc.alloc_semaphore("m_" + k), 0] for k in ("pe", "dve", "act", "pool")}
        self.cc_sems = []
        self.cc_n = 0
        self.par = {}

    def em(self, eng, fn):
        self.ops[eng].append(fn)

    def mark(self, key):
        s = self.cnt[key]
        s[1] += 1
        sem, val = s[0], s[1]
        eng = ENG_OF[key]
        lst = self.ops[eng]
        if lst and not getattr(lst[-1], "_is_wait", False):
            prev = lst.pop()

            def wrapped(e, prev=prev, s_=sem, v=val):
                r = prev(e)
                if hasattr(r, "then_inc"):
                    r.then_inc(s_, 1)
                else:
                    e.nop().then_inc(s_, 1)
            lst.append(wrapped)
        else:
            self.em(eng, lambda e, s_=sem, v=val: e.nop().then_inc(s_, 1))
        return ("cnt", sem, val)

    def cur(self, key):
        s = self.cnt[key]
        return ("cnt", s[0], s[1])

    def _emw(self, eng, fn):
        fn._is_wait = True
        self.ops[eng].append(fn)

    def wait(self, eng, *gates):
        for g in gates:
            if g is None:
                continue
            kind = g[0]
            if kind == "cnt":
                _, sem, val = g
                if val > 0:
                    self._emw(eng, lambda e, s=sem, v=val: e.wait_ge(s, v))
            elif kind == "cc":
                _, val = g
                for k in range(val):
                    self._emw(eng, lambda e, s=self.cc_sems[k]: e.wait_ge(s, 1))
            elif kind == "dma":
                _, sems, c0, c1 = g
                if c0 > 0:
                    self._emw(eng, lambda e, s=sems[0], v=16 * c0: e.wait_ge(s, v))
                if c1 > 0:
                    self._emw(eng, lambda e, s=sems[1], v=16 * c1: e.wait_ge(s, v))
            else:
                raise ValueError(kind)


class Stream:
    def __init__(self, kb, name, queue):
        self.kb = kb
        self.q = queue
        self.sems = [kb.nc.alloc_semaphore(name + "0"), kb.nc.alloc_semaphore(name + "1")]
        self.c = [0, 0]
        self.k = 0

    def issue(self, mk, gates=()):
        par = self.k % 2
        self.c[par] += 1
        self.kb.wait(self.q, *gates)
        sem = self.sems[par]
        self.kb.em(self.q, lambda e, s=sem, m=mk: m(e).then_inc(s, 16))
        self.k += 1
        return ("dma", self.sems, self.c[0], self.c[1])


def build(upto=9, taps=()):
    kb = Kb()
    nc = kb.nc
    em, mark, curc, wait = kb.em, kb.mark, kb.cur, kb.wait

    def din(name, shape, dt):
        return nc.dram_tensor(name, shape, dt, kind="ExternalInput")

    def dint(name, shape, dt, **kw):
        kind = "ExternalOutput" if name in taps else "Internal"
        return nc.dram_tensor(name, shape, dt, kind=kind, **kw)

    xT = din("xT", [H, KT], F32)
    cosq = din("cosq", [HD, T], BF16)
    sinq = din("sinq", [HD, T], BF16)
    cosk = din("cosk", [HD, KT], BF16)
    sink = din("sink", [HD, KT], BF16)
    masks = din("masks", [8, 128, 640], BF16)
    idbf = din("idbf", [128, 128], BF16)
    idf = din("idf", [128, 128], F32)
    onesf = din("onesf", [128, 128], F32)
    Wq = din("Wq", [32, H, 128], F32)
    Wk = din("Wk", [8, H, 128], F32)
    Wv = din("Wv", [8, H, 128], F32)
    Wo = din("Wo", [32, H, 128], F32)
    w1q = din("w1q", [1408, H], F32)
    w2q = din("w2q", [1408, H], F32)
    w3q = din("w3q", [H, 2048], F32)
    y = nc.dram_tensor("y", [T, H], F32, kind="ExternalOutput")

    slab12 = dint("slab12", [2, NLC, NDK, 128, 128], BF16)
    slab3 = dint("slab3", [NDK, NLC, 128, 128], BF16)
    ghw = {(0, 0): dint("gh10", [4, 6, NDK, 128, 128], BF16),
           (0, 1): dint("gh11", [4, 5, NDK, 128, 128], BF16),
           (1, 0): dint("gh20", [4, 6, NDK, 128, 128], BF16),
           (1, 1): dint("gh21", [4, 5, NDK, 128, 128], BF16)}
    gh3a = dint("gh3a", [4, 16, NLC, 128, 128], BF16)
    gh3b = dint("gh3b", [4, 16, NLC, 128, 128], BF16)
    adr = dint("adr", [8, 32, 128, 128], BF16)
    xmidf = dint("xmidf", [32, 128, T], F32)
    xstage = dint("xstage", [32, 128, T], BF16)
    pstage = dint("pstage", [2, 32, 128, T], F32)
    prstage = dint("prstage", [2, 32, 128, T], F32)
    xsh = nc.dram_tensor("xsh", [2, 32, 128, T], BF16, addr_space="Shared")
    pall = nc.dram_tensor("pall", [2, 2, 32, 128, T], F32, addr_space="Shared")
    tap_k = dint("tap_k", [128, KT], BF16)
    tap_vt = dint("tap_vt", [128, 1024], BF16)
    tap_q = dint("tap_q", [128, 512], BF16)
    tap_xg = dint("tap_xg", [128, 512], BF16)
    tap_wq = dint("tap_wq", [128, 4096], BF16)
    tap_ps = dint("tap_ps", [128, 512], F32)
    tap_p = dint("tap_p", [128, 640], BF16)
    tap_sm = dint("tap_sm", [128, 640], F32)
    tap_m = dint("tap_m", [128, 5120], BF16)
    tap_hact = dint("tap_hact", [43, 128, 512], BF16)
    tap_v = dint("tap_v", [128, 1024], F32)
    tap_in = dint("tap_in", [128, 1024], F32)
    tap_am = dint("tap_am", [128, 64], F32)
    tap_bf = dint("tap_bf", [128, 4096], BF16)
    ccin = nc.dram_tensor("ccin", [1, 4], F32)
    ccout = nc.dram_tensor("ccout", [1, 4], F32)

    GROUPS4 = [[0, 2, 4, 6], [1, 3, 5, 7]]
    PAIRS = [[2 * i, 2 * i + 1] for i in range(4)]

    ident_bf = nc.alloc_sbuf_tensor("ident_bf", [128, 128], BF16)
    ident_f = nc.alloc_sbuf_tensor("ident_f", [128, 128], F32)
    ones_f = nc.alloc_sbuf_tensor("ones_f", [128, 128], F32)
    ARENA = 206 * 1024
    arena = nc.alloc_sbuf_tensor("arena", [128, ARENA], mybir.dt.uint8)

    def carve(off, shape, dt):
        esz = 2 if dt == BF16 else 4
        n = int(np.prod(shape[1:]))
        assert off % 4 == 0
        end = off + n * esz
        assert end <= ARENA, (off, shape, end)
        v = arena[:, off:end].bitcast(dt)
        if len(shape) == 3:
            v = v.rearrange("p (a b) -> p a b", b=shape[2])
        return v, end

    def carve_list(off, count, shape, dt):
        out = []
        for _ in range(count):
            t, off = carve(off, shape, dt)
            out.append(t)
        return out, off

    o = 0
    dq_in0, o = carve(o, [128, 1024], F32)
    dq_in1, o = carve(o, [128, 1024], F32)
    dq_v, o = carve(o, [128, 1024], F32)
    dq_ind, o = carve(o, [128, 1024], BF16)
    dq_a0, o = carve(o, [128, 1024], BF16)
    dq_a1, o = carve(o, [128, 1024], BF16)
    dq_am, o = carve(o, [128, 64], F32)
    dq_am2, o = carve(o, [128, 64], F32)
    dq_rcp, o = carve(o, [128, 64], F32)
    dq_bf, o = carve(o, [128, 4096], BF16)
    dq_ts, o = carve(o, [128, 256], BF16)
    DQ_END = (o + 3) // 4 * 4

    ROPE_OFF = ARENA - (2 * 2 * T + 2 * 2 * KT)
    o = ROPE_OFF
    cosq_sb, o = carve(o, [128, T], BF16)
    sinq_sb, o = carve(o, [128, T], BF16)
    cosk_sb, o = carve(o, [128, KT], BF16)
    sink_sb, o = carve(o, [128, KT], BF16)
    assert o == ARENA

    o = DQ_END
    Kfm, o = carve_list(o, NKV, [128, KT], BF16)
    Vtm, o = carve_list(o, 12, [128, NKV * HD], BF16)
    KV_END = (o + 3) // 4 * 4

    o = KV_END
    xbf, o = carve_list(o, 32, [128, KT], BF16)
    wf0, o = carve(o, [128, 1024], F32)
    wf1, o = carve(o, [128, 1024], F32)
    wbf, o = carve(o, [128, 4096], BF16)
    vfm, o = carve(o, [128, KT], BF16)
    rt1, o = carve(o, [128, 512], F32)
    rt2, o = carve(o, [128, 512], F32)
    assert o <= ROPE_OFF, o

    o = KV_END
    xg, o = carve_list(o, 32, [128, 512], BF16)
    qbuf, o = carve_list(o, 32, [128, 512], BF16)
    wqf0, o = carve(o, [128, 1024], F32)
    wqf1, o = carve(o, [128, 1024], F32)
    wqbf, o = carve(o, [128, 4096], BF16)
    wqbf2, o = carve(o, [128, 4096], BF16)
    o_qrt = o
    q_rt1, o = carve(o, [128, 512], F32)
    q_rt2, o = carve(o, [128, 512], F32)
    sm_f32b = arena[:, o_qrt:o_qrt + 2560].bitcast(F32)
    exp_bfb = arena[:, o_qrt + 2560:o_qrt + 3840].bitcast(BF16)
    mask_sb, o = carve(o, [128, 2 * 640], BF16)
    exp_bf, o = carve(o, [128, 640], BF16)
    pm_bf, o = carve(o, [128, 640], BF16)
    sm_f32, o = carve(o, [128, 640], F32)
    pt_sb, o = carve(o, [128, 640], BF16)
    negmax, o = carve(o, [128, 4], F32)
    ssum, o = carve(o, [128, 64], F32)
    srecip, o = carve(o, [128, 64], F32)
    astage0, o = carve(o, [128, 32 * 128], BF16)
    astage1, o = carve(o, [128, 32 * 128], BF16)
    assert o <= ROPE_OFF, o

    o = DQ_END
    ah, o = carve_list(o, 32, [128, 512], BF16)
    res1, o = carve_list(o, 32, [128, 512], F32)
    wof0, o = carve(o, [128, 1024], F32)
    wof1, o = carve(o, [128, 1024], F32)
    wobf, o = carve(o, [128, 4096], BF16)
    wobf2, o = carve(o, [128, 4096], BF16)
    xr0, o = carve(o, [128, 512], F32)
    xr1, o = carve(o, [128, 512], F32)
    sqt, o = carve(o, [128, 512], F32)
    ln_mean, o = carve(o, [128, 512], F32)
    ln_var, o = carve(o, [128, 512], F32)
    ln_rstd, o = carve(o, [128, 512], F32)
    ln_std, o = carve(o, [128, 512], F32)
    ln_b, o = carve(o, [128, 512], F32)
    xmid_c, o = carve(o, [128, 512], F32)
    assert o <= ROPE_OFF, o

    o = DQ_END
    xmg, o = carve_list(o, 32, [128, 512], BF16)
    hact, o = carve_list(o, NHC, [128, 512], BF16)
    w1t0, o = carve(o, [128, 32 * 128], BF16)
    w1t1, o = carve(o, [128, 32 * 128], BF16)
    w2t0, o = carve(o, [128, 32 * 128], BF16)
    w2t1, o = carve(o, [128, 32 * 128], BF16)
    w3t0, o = carve(o, [128, 44 * 128], BF16)
    w3t1, o = carve(o, [128, 44 * 128], BF16)
    silu_t, o = carve(o, [128, 512], F32)
    dstage0, o = carve(o, [128, 512], F32)
    dstage1, o = carve(o, [128, 512], F32)
    assert o <= ROPE_OFF, o

    o = DQ_END
    res2, o = carve_list(o, 32, [128, 512], F32)
    p_in0, o = carve(o, [128, 512], F32)
    p_in1, o = carve(o, [128, 512], F32)
    p_in2, o = carve(o, [128, 512], F32)
    l2_sq, o = carve(o, [128, 512], F32)
    l2_mean, o = carve(o, [128, 512], F32)
    l2_var, o = carve(o, [128, 512], F32)
    l2_rstd, o = carve(o, [128, 512], F32)
    l2_std, o = carve(o, [128, 512], F32)
    l2_b, o = carve(o, [128, 512], F32)
    onorm, o = carve(o, [128, 512], F32)
    ostage, o = carve(o, [128, 4 * 128], F32)
    assert o <= ROPE_OFF, o

    ps_a = nc.alloc_psum_tensor("ps_a", [128, 512], F32)
    ps_b = nc.alloc_psum_tensor("ps_b", [128, 512], F32)
    psc = nc.alloc_psum_tensor("psc", [128, 640], F32)
    pt = nc.alloc_psum_tensor("pt", [128, 1024], BF16)
    ps_e = nc.alloc_psum_tensor("ps_e", [128, 512], F32)
    st_a = nc.alloc_psum_tensor("st_a", [128, 512], F32)
    st_b = nc.alloc_psum_tensor("st_b", [128, 512], F32)

    TB_67 = (st_a.bitcast(BF16)[:, 0:128], st_b.bitcast(BF16)[:, 0:128])
    TB_23 = (psc.bitcast(BF16)[:, 0:128], psc.bitcast(BF16)[:, 1024:1152])
    VT_42 = (pt[:, 640:768], psc.bitcast(BF16)[:, 0:128])

    S_wsl = Stream(kb, "wsl", "sync")
    S_w = Stream(kb, "w", "sync")
    S_x = Stream(kb, "x", "gpsimd")
    S_misc = Stream(kb, "misc", "sync")
    S_dqst = Stream(kb, "dqst", "scalar")
    S_aw = Stream(kb, "aw", "scalar")
    S_ar = Stream(kb, "ar", "sync")
    S_xr = Stream(kb, "xr", "sync")
    S_xmw = Stream(kb, "xmw", "sync")
    S_xshw = Stream(kb, "xshw", "gpsimd")
    S_gw = Stream(kb, "gw", "sync")
    S_g3 = Stream(kb, "g3", "scalar")
    S_xmr = Stream(kb, "xmr", "sync")
    S_pw = Stream(kb, "pw", "gpsimd")
    S_pr = Stream(kb, "pr", "gpsimd")
    S_xfr = Stream(kb, "xfr", "sync")
    S_out = Stream(kb, "out", "scalar")

    def pool_prolog(g):
        cid = nc.partition_id([ET.Pool])
        kb.par["j"] = nc.snap(cid % 2, [ET.Pool], min_val=0, max_val=1)

    em("gpsimd", pool_prolog)

    t_id = S_misc.issue(lambda e: e.dma_start(out=ident_bf[:, :], in_=idbf[:, :]))
    S_misc.issue(lambda e: e.dma_start(out=ident_f[:, :], in_=idf[:, :]))
    t_ones = S_misc.issue(lambda e: e.dma_start(out=ones_f[:, :], in_=onesf[:, :]))
    S_misc.issue(lambda e: e.dma_start(out=cosq_sb, in_=cosq[:, :]))
    S_misc.issue(lambda e: e.dma_start(out=sinq_sb, in_=sinq[:, :]))
    S_misc.issue(lambda e: e.dma_start(out=cosk_sb, in_=cosk[:, :]))
    t_rope = S_misc.issue(lambda e: e.dma_start(out=sink_sb, in_=sink[:, :]))

    class DQ:
        def __init__(self):
            self.units = []
            for lc in range(NLC):
                self.units.append((0, lc))
                self.units.append((1, lc))
            for dk in range(NDK):
                self.units.append((2, dk))
            self.ui = 0
            self.load_toks = {}
            self.load_gate = {}
            self.li = 0
            self.store_tok = {}
            self.ev = {}
            for _ in range(2):
                self._load_next()

        def nsubs(self, u):
            return 4 if self.units[u][0] < 2 else 2

        def _sub_of(self, li):
            u = 0
            while True:
                ns = self.nsubs(u)
                if li < ns:
                    return u, li
                li -= ns
                u += 1

        def total_subs(self):
            return sum(self.nsubs(u) for u in range(len(self.units)))

        def _load_next(self):
            if self.li >= self.total_subs():
                return
            u, su = self._sub_of(self.li)
            w, lc = self.units[u]
            buf = dq_in0 if (self.li % 2 == 0) else dq_in1
            gates = [self.load_gate.get(self.li - 2)]
            if w < 2:
                src = (w1q if w == 0 else w2q)[lc * 128:(lc + 1) * 128, su * 1024:(su + 1) * 1024]
            else:
                src = w3q[lc * 128:(lc + 1) * 128, su * 1024:(su + 1) * 1024]
            tok = S_wsl.issue(lambda e, s=src, b=buf: e.dma_start(out=b[:, :], in_=s), gates=gates)
            self.load_toks[self.li] = tok
            self.li += 1

        def pump(self, n, tbanks):
            if DQ_DISABLE:
                return
            done = 0
            while done < n and self.ui < len(self.units):
                u = self.ui
                w, lc = self.units[u]
                ns = self.nsubs(u)
                base_li = sum(self.nsubs(x) for x in range(u))
                for su in range(ns):
                    li = base_li + su
                    while self.li <= li:
                        self._load_next()
                    self._emit_sub(u, su, li)
                    self._load_next()
                self._emit_T(u, tbanks)
                self.ui += 1
                done += 1

        def _emit_sub(self, u, su, li):
            w, lc = self.units[u]
            FD = 1024
            NB = 16
            buf = dq_in0 if (li % 2 == 0) else dq_in1
            wait("vector", self.load_toks[li], getattr(self, "last_T", None))

            def dve_fence():
                dmk = mark("dve")
                wait("gpsimd", dmk)
                em("gpsimd", lambda g: g.nop())
                pmk = mark("pool")
                wait("vector", pmk)

            def blk1a(v, buf=buf, NB=NB, FD=FD):
                w3v = buf[:, 0:FD].rearrange("p (nb bl) -> p nb bl", bl=64)
                return v.reduce_max(dq_am[:, 0:NB].unsqueeze(2), w3v, axis=AX.X,
                                    apply_absolute_value=True)

            def blk1b(v, FD=FD):
                v.memset(dq_a0[:, 0:FD], CODE0_BF)
                return v.tensor_scalar_max(dq_am2[:, 0:64], dq_am[:, 0:64], 1e-38)

            def blk2(v):
                v.memset(dq_ind[:, 0:64], 0.0)
                return v.reciprocal(dq_rcp[:, 0:64], dq_am2[:, 0:64])

            def blk3(v, buf=buf, FD=FD, NB=NB, su=su):
                w3v = buf[:, 0:FD].rearrange("p (nb bl) -> p nb bl", bl=64)
                rb = dq_rcp[:, 0:NB].unsqueeze(2).broadcast_to([128, NB, 64])
                v.tensor_tensor(dq_v[:, 0:FD].rearrange("p (nb bl) -> p nb bl", bl=64), w3v, rb, ALU.mult)
                bufs = [dq_a0, dq_a1]
                c = 0
                for bnd in range(0, 15):
                    v.tensor_scalar(dq_ind[:, 0:FD], dq_v[:, 0:FD], float(NF4_BOUNDS[bnd]),
                                    float(DELTAS_BF[bnd]), ALU.is_gt, ALU.mult)
                    v.tensor_tensor(bufs[1 - c][:, 0:FD], dq_ind[:, 0:FD], bufs[c][:, 0:FD], ALU.add)
                    c = 1 - c
                ab = dq_am2[:, 0:NB].unsqueeze(2).broadcast_to([128, NB, 64])
                return v.tensor_tensor(
                    dq_bf[:, su * FD:(su + 1) * FD].rearrange("p (nb bl) -> p nb bl", bl=64),
                    bufs[c][:, 0:FD].rearrange("p (nb bl) -> p nb bl", bl=64), ab, ALU.mult)

            em("vector", blk1a)
            dve_fence()
            em("vector", blk1b)
            dve_fence()
            em("vector", blk2)
            dve_fence()
            em("vector", blk3)
            self.load_gate[li] = self.sub_done = mark("dve")
            if DQ_TAP == (u, su):
                S_misc.issue(lambda e, b=buf: e.dma_start(out=tap_in[:, :], in_=b[:, :]),
                             gates=[self.sub_done])
                S_misc.issue(lambda e: e.dma_start(out=tap_v[:, :], in_=dq_v[:, :]))
                S_misc.issue(lambda e: e.dma_start(out=tap_am[:, :], in_=dq_am2[:, :]))
                S_misc.issue(lambda e: e.dma_start(out=tap_bf[:, :], in_=dq_bf[:, :]))

        def _emit_T(self, u, tbanks):
            w, lc = self.units[u]
            wait("tensor", self.sub_done)
            ntile = NDK if w < 2 else NLC
            stok = None
            for t in range(ntile):
                sl = t % 2
                bank = tbanks[sl]
                wait("tensor", self.ev.get((id(tbanks[0]), sl)))
                em("tensor", lambda te, b=bank, s=dq_bf[:, t * 128:(t + 1) * 128]:
                   te.transpose(b, s, ident_bf[:, :]))
                pm = mark("pe")
                wait("scalar", pm)
                stg = dq_ts[:, sl * 128:(sl + 1) * 128]
                wait("scalar", self.stg_rd.get(sl) if hasattr(self, "stg_rd") else None)
                em("scalar", lambda sc, b=bank, d=stg: sc.activation(out=d, in_=b, func=AF.Copy))
                av = mark("act")
                self.ev[(id(tbanks[0]), sl)] = av
                dst = slab12[w, lc, t] if w < 2 else slab3[lc, t]
                stok = S_dqst.issue(lambda e, d=dst, s=stg: e.dma_start(out=d, in_=s), gates=[av])
                if not hasattr(self, "stg_rd"):
                    self.stg_rd = {}
                self.stg_rd[sl] = stok
                self.last_T_pe = pm
            self.last_T = self.last_T_pe
            self.store_tok[u] = stok

    dq = DQ()

    def u_idx(w, lc):
        return dq.units.index((w, lc) if w < 2 else (2, lc))

    def emit_ag(region_in, region_out, after_units):
        for u in after_units:
            wait("gpsimd", dq.store_tok[u])
        sem = nc.alloc_semaphore(f"cc{kb.cc_n}")
        kb.cc_sems.append(sem)
        em("gpsimd", lambda g, i=region_in, o_=region_out, s=sem: g.collective_compute(
            "AllGather", ALU.bypass, replica_groups=GROUPS4, ins=[i], outs=[o_]).then_inc(s, 1))
        kb.cc_n += 1
        return ("cc", kb.cc_n)

    def emit_pair_barrier():
        sem = nc.alloc_semaphore(f"cc{kb.cc_n}")
        kb.cc_sems.append(sem)
        em("gpsimd", lambda g, s=sem: g.collective_compute(
            "AllReduce", ALU.add, replica_groups=PAIRS,
            ins=[ccin.ap().opt()], outs=[ccout.ap().opt()]).then_inc(s, 1))
        kb.cc_n += 1
        return ("cc", kb.cc_n)

    def flat12(w, l0, l1):
        return slab12[w, l0:l1].rearrange("a b p f -> (a b p) f")

    def gflat12(w, part):
        return ghw[(w, part)].ap().rearrange("r a b p f -> (r a b p) f")


    def wload(stream, blk, f0, f1, bfdst, wg, pre_cast_gate=None):
        marks = [wg[0], wg[1]]
        for qd in range(4):
            fb = (f0, f1)[qd % 2]
            tok = stream.issue(lambda e, s=blk, qd=qd, fb=fb: e.dma_start(
                out=fb[:, :].rearrange("p (a f) -> p a f", f=128),
                in_=s[qd * 1024:(qd + 1) * 1024, :].rearrange(
                    "(a p) f -> p a f", p=128)), gates=[marks[qd % 2]])
            if qd == 0 and pre_cast_gate is not None:
                wait("scalar", pre_cast_gate)
            wait("scalar", tok)
            em("scalar", lambda sc, fb=fb, qd=qd: sc.activation(
                out=bfdst[:, qd * 1024:(qd + 1) * 1024], in_=fb[:, :], func=AF.Copy))
            marks[qd % 2] = mark("act")
        return marks

    wait("tensor", t_rope)
    wait("vector", t_rope)

    em("vector", lambda v: v.memset(dq_am[:, :], 1.0))
    em("vector", lambda v: v.memset(dq_am2[:, :], 1.0))
    em("vector", lambda v: v.memset(ssum[:, :], 1.0))

    x_toks = []
    for c in range(32):
        x_toks.append(S_x.issue(lambda e, c=c: e.dma_start(
            out=xbf[c][:, :], in_=xT[c * 128:(c + 1) * 128, :])))

    wgate = [None, None]
    accum_guard = {0: None, 1: None}
    vt_guard = {0: None, 1: None}
    kv_pm = [None]
    for dc in range(min(2 * NKV, P1_CHUNKS)):
        is_v = dc >= NKV
        kvc = dc - NKV if is_v else dc
        wt = Wv if is_v else Wk
        wgate = wload(S_w, wt[kvc], wf0, wf1, wbf, wgate,
                      pre_cast_gate=(kv_pm[0] if dc > 0 else None))
        ca, cb = wgate
        for sl in range(3):
            bank = [ps_a, ps_b][sl % 2]
            wait("tensor", accum_guard[sl % 2])
            if sl == 0:
                wait("tensor", ca, cb)
                if dc == 0:
                    wait("tensor", x_toks[-1])
            for dnc in range(32):
                em("tensor", lambda te, l=wbf[:, dnc * 128:(dnc + 1) * 128],
                   r=xbf[dnc][:, sl * 512:(sl + 1) * 512], b=bank, st=(dnc == 0), sp=(dnc == 31):
                   te.matmul(b[:, :], l, r, start=st, stop=sp))
            pm = mark("pe")
            kv_pm[0] = pm
            if not is_v:
                wait("vector", pm)
                ck = cosk_sb[:, sl * 512:(sl + 1) * 512]
                sk0 = sink_sb[0:64, sl * 512:(sl + 1) * 512]
                sk1 = sink_sb[64:128, sl * 512:(sl + 1) * 512]
                em("vector", lambda v, b=bank, c=ck: v.tensor_tensor(rt1[:, :], b[:, :], c, ALU.mult))
                em("vector", lambda v, b=bank, s=sk0: v.tensor_tensor(rt2[0:64, :], b[64:128, :], s, ALU.mult))
                em("vector", lambda v, b=bank, s=sk1: v.tensor_tensor(rt2[64:128, :], b[0:64, :], s, ALU.mult))
                em("vector", lambda v, d=Kfm[kvc][:, sl * 512:(sl + 1) * 512]:
                   v.tensor_tensor(d, rt1[:, :], rt2[:, :], ALU.add))
                accum_guard[sl % 2] = mark("dve")
            else:
                wait("scalar", pm)
                em("scalar", lambda sc, b=bank, s=sl: sc.activation(
                    out=vfm[:, s * 512:(s + 1) * 512], in_=b[:, :], func=AF.Copy))
                accum_guard[sl % 2] = mark("act")
        if is_v:
            vev = curc("act")
            for tt in range(12):
                sl = tt % 2
                bank = VT_42[sl]
                wait("tensor", vt_guard[sl])
                if tt == 0:
                    wait("tensor", vev)
                em("tensor", lambda te, b=bank, s=vfm[:, tt * 128:(tt + 1) * 128]:
                   te.transpose(b, s, ident_bf[:, :]))
                pm2 = mark("pe")
                wait("scalar", pm2)
                em("scalar", lambda sc, b=bank, d=Vtm[tt][:, kvc * 128:(kvc + 1) * 128]:
                   sc.activation(out=d, in_=b, func=AF.Copy))
                vt_guard[sl] = mark("act")
        if ATTN_TAP and dc == 0:
            S_misc.issue(lambda e: e.dma_start(out=tap_k[:, :], in_=Kfm[0][:, :]),
                         gates=[curc("dve")])
        if ATTN_TAP and dc == NKV:
            S_misc.issue(lambda e: e.dma_start(out=tap_vt[:, :], in_=Vtm[5][:, :]),
                         gates=[curc("act")])
        dq.pump(1, TB_67)

    kv_pe = curc("pe")
    kv_act = curc("act")
    dq.pump(2, TB_67)

    if upto <= 1:
        dq.pump(1000, TB_67)
        if not DQ_DISABLE:
            wait("scalar", dq.store_tok[len(dq.units) - 1])
        elif P1_CHUNKS > 0:
            wait("scalar", curc("act"))
        return _materialize(kb, nc)

    assert dq.ui >= 12
    ag12a = ag12a2 = ag12b = ag12b2 = ("cc", 0)
    if upto >= 4:
        ag12a = emit_ag(flat12(0, 0, 6), gflat12(0, 0), [u_idx(0, lc) for lc in range(6)])
        ag12a2 = emit_ag(flat12(1, 0, 6), gflat12(1, 0), [u_idx(1, lc) for lc in range(6)])

    qgate = [None, None]
    msk_last_use = {0: kv_pe, 1: kv_pe}
    msk_tok = {}
    rope_guard = {0: None, 1: None}
    stage_rd = {0: None, 1: None}
    attn_w_tok = {}
    ptg = [None]
    pvg = [None]
    prev_ae = {0: None, 1: None}
    prev_pmt = {0: None, 1: None}
    qproj_pe = [kv_pe]

    for g in range(2):
        xg_toks = []
        for c in range(32):
            xg_toks.append(S_x.issue(lambda e, c=c, g=g: e.dma_start(
                out=xg[c][:, :], in_=xT[c * 128:(c + 1) * 128, 512 + g * 512:1024 + g * 512]),
                gates=[qproj_pe[0]]))
        qpm = {}
        for hc in range(32):
            if g == 0 and hc == 0:
                qgate = [kv_pe, kv_pe]
            wqb = [wqbf, wqbf2][hc % 2]
            qgate = wload(S_w, Wq[hc], wqf0, wqf1, wqb, qgate,
                          pre_cast_gate=qpm.get(hc - 2))
            ca, cb = qgate
            bank = [ps_a, ps_b][hc % 2]
            wait("tensor", rope_guard[hc % 2], ca, cb)
            if hc == 0:
                wait("tensor", xg_toks[-1])
            for dnc in range(32):
                em("tensor", lambda te, l=wqb[:, dnc * 128:(dnc + 1) * 128], r=xg[dnc][:, :],
                   b=bank, st=(dnc == 0), sp=(dnc == 31): te.matmul(b[:, :], l, r, start=st, stop=sp))
            pm = mark("pe")
            qpm[hc] = pm
            wait("vector", pm)
            if g == 1 and hc == 0:
                wait("vector", prev_ae[1], prev_pmt[1])
            cqs = cosq_sb[:, g * 512:(g + 1) * 512]
            sq0 = sinq_sb[0:64, g * 512:(g + 1) * 512]
            sq1 = sinq_sb[64:128, g * 512:(g + 1) * 512]
            em("vector", lambda v, b=bank, c=cqs: v.tensor_tensor(q_rt1[:, :], b[:, :], c, ALU.mult))
            em("vector", lambda v, b=bank, s=sq0: v.tensor_tensor(q_rt2[0:64, :], b[64:128, :], s, ALU.mult))
            em("vector", lambda v, b=bank, s=sq1: v.tensor_tensor(q_rt2[64:128, :], b[0:64, :], s, ALU.mult))
            em("vector", lambda v, d=qbuf[hc][:, :]: v.tensor_tensor(d, q_rt1[:, :], q_rt2[:, :], ALU.add))
            if ATTN_TAP and g == 1 and hc == 11:
                em("vector", lambda v, b=bank: v.tensor_copy(sm_f32[:, 0:512], b[:, :]))
            rope_guard[hc % 2] = mark("dve")
            if ATTN_TAP and g == 1 and hc == 11:
                S_misc.issue(lambda e: e.dma_start(out=tap_q[:, :], in_=qbuf[11][:, :]),
                             gates=[rope_guard[hc % 2]])
                S_misc.issue(lambda e: e.dma_start(out=tap_xg[:, :], in_=xg[5][:, :]))
                S_misc.issue(lambda e: e.dma_start(out=tap_wq[:, :], in_=wqbf[:, :]))
                S_misc.issue(lambda e: e.dma_start(out=tap_ps[:, :], in_=sm_f32[:, 0:512]))
            if hc % 8 == 7:
                dq.pump(1, TB_67)
        qproj_pe[0] = curc("pe")
        rope_all = curc("dve")
        wait("tensor", rope_all)
        for qt in range(4 * g, 4 * g + 4):
            qoff = (qt - 4 * g) * 128
            msl = qt % 2
            msk_tok[msl] = S_misc.issue(lambda e, qt=qt, msl=msl: e.dma_start(
                out=mask_sb[:, msl * 640:(msl + 1) * 640], in_=masks[qt]),
                gates=[msk_last_use[msl]])
            sc_cons = None
            for h in range(NH):
                kv = h // GQ
                par = h % 2
                smb = [sm_f32, sm_f32b][par]
                expb = [exp_bf, exp_bfb][par]
                nmx = negmax[:, par * 2:par * 2 + 1]
                wait("tensor", sc_cons)
                em("tensor", lambda te, l=qbuf[h][:, qoff:qoff + 128], r=Kfm[kv][:, qt * 128:qt * 128 + 640]:
                   te.matmul(psc[:, 0:512], l, r[:, 0:512], start=True, stop=True))
                em("tensor", lambda te, l=qbuf[h][:, qoff:qoff + 128], r=Kfm[kv][:, qt * 128:qt * 128 + 640]:
                   te.matmul(psc[:, 512:640], l, r[:, 512:640], start=True, stop=True))
                pm = mark("pe")
                wait("vector", pm, prev_ae[par])
                if h == 0:
                    wait("vector", msk_tok[msl])
                mk = mask_sb[:, msl * 640:(msl + 1) * 640]
                em("vector", lambda v, m=mk, s=smb: v.tensor_tensor(s[:, :], psc[:, :], m, ALU.add))
                em("vector", lambda v, s=smb, n=nmx: v.reduce_max(n, s[:, :], axis=AX.X, negate=True))
                sc_cons = dm = mark("dve")
                if h == NH - 1:
                    msk_last_use[msl] = dm
                wait("scalar", dm)
                em("scalar", lambda sc, s=smb, n=nmx: sc.activation(
                    out=pm_bf[:, :], in_=s[:, :], func=AF.Exp,
                    bias=n, scale=1.0, accum_out=ssum[:, 0:1]))
                ae = mark("act")
                prev_ae[par] = ae
                wait("vector", ae)
                em("vector", lambda v: v.reciprocal(srecip[:, 0:64], ssum[:, 0:64]))
                dn = mark("dve")
                wait("scalar", dn, prev_pmt[par])
                em("scalar", lambda sc, d=expb: sc.activation(out=d[:, :], in_=pm_bf[:, :], func=AF.Copy,
                                                              scale=srecip[:, 0:1]))
                dn = mark("act")
                if ATTN_TAP and qt == 4 and h == 0:
                    tt1 = S_misc.issue(lambda e: e.dma_start(out=tap_p[:, :], in_=exp_bf[:, :]), gates=[dn])
                    tt2 = S_misc.issue(lambda e: e.dma_start(out=tap_sm[:, :], in_=sm_f32[:, :]))
                    pass
                wait("tensor", dn, ptg[0])
                for c5 in range(5):
                    em("tensor", lambda te, c=c5, s=expb: te.transpose(
                        pt[:, c * 128:(c + 1) * 128], s[:, c * 128:(c + 1) * 128], ident_bf[:, :]))
                pmt = mark("pe")
                prev_pmt[par] = pmt
                wait("scalar", pmt)
                em("scalar", lambda sc: sc.activation(out=pt_sb[:, :], in_=pt[:, 0:640], func=AF.Copy))
                ptg[0] = mark("act")
                wait("tensor", ptg[0], pvg[0])
                for c5 in range(5):
                    em("tensor", lambda te, c=c5, kv=kv, qt=qt: te.matmul(
                        ps_e[:, 0:128], Vtm[qt + c][:, kv * 128:(kv + 1) * 128],
                        pt_sb[:, c * 128:(c + 1) * 128], start=(c == 0), stop=(c == 4)))
                pvm = mark("pe")
                wait("scalar", pvm)
                stg = [astage0, astage1][qt % 2]
                if h == 0:
                    wait("scalar", stage_rd[qt % 2])
                em("scalar", lambda sc, d=stg, h=h: sc.activation(
                    out=d[:, h * 128:(h + 1) * 128], in_=ps_e[:, 0:128], func=AF.Copy))
                pvg[0] = mark("act")
                if h % 8 == 7:
                    dq.pump(1, TB_67)
            last_tok = None
            for hcc in range(32):
                stg = [astage0, astage1][qt % 2]
                last_tok = S_aw.issue(lambda e, d=adr[qt, hcc], s=stg[:, hcc * 128:(hcc + 1) * 128]:
                                      e.dma_start(out=d, in_=s), gates=[pvg[0] if hcc == 0 else None])
            stage_rd[qt % 2] = last_tok
            attn_w_tok[qt] = last_tok
        if g == 0 and upto >= 4:
            assert dq.ui >= 22, dq.ui
            ag12b = emit_ag(flat12(0, 6, NLC), gflat12(0, 1), [u_idx(0, lc) for lc in range(6, NLC)])
            ag12b2 = emit_ag(flat12(1, 6, NLC), gflat12(1, 1), [u_idx(1, lc) for lc in range(6, NLC)])

    p2_pe = curc("pe")
    p2_act = curc("act")
    p2_dve = curc("dve")
    dq.pump(3, TB_67)
    if upto <= 2:
        dq.pump(1000, TB_67)
        wait("scalar", dq.store_tok[len(dq.units) - 1], stage_rd[0], stage_rd[1])
        return _materialize(kb, nc)

    xsh_w_last = None
    wog = [None, None]
    prev_half_pe = [p2_pe]
    stats_rd = [None]
    for half in range(2):
        a_last = None
        for c in range(32):
            for tt in range(4):
                a_last = S_ar.issue(lambda e, c=c, tt=tt, half=half: e.dma_start(
                    out=ah[c][:, tt * 128:(tt + 1) * 128], in_=adr[4 * half + tt, c]),
                    gates=[attn_w_tok[4 * half + tt] if c == 0 else None,
                           p2_pe if (c == 0 and tt == 0 and half == 0) else None,
                           prev_half_pe[0] if (c == 0 and tt == 0 and half == 1) else None])
        ev_guard = {0: None, 1: None}
        stm = None
        wopm = {}
        prev_stm = [None]
        for dc in range(32):
            if half == 0 and dc == 0:
                wog = [p2_pe, p2_pe]
                wait("scalar", stage_rd[0], stage_rd[1])
                wait("vector", stage_rd[0], stage_rd[1], p2_act)
            wob = [wobf, wobf2][dc % 2]
            wog = wload(S_w, Wo[dc], wof0, wof1, wob, wog,
                        pre_cast_gate=(wopm.get(dc - 2) if dc >= 2 else
                                       (p2_pe if half == 0 else prev_half_pe[0])))
            ca, cb = wog
            xrb = [xr0, xr1][dc % 2]
            xtok = S_xr.issue(lambda e, dc=dc, half=half, d=xrb: e.dma_start(
                out=d[:, :], in_=xT[dc * 128:(dc + 1) * 128, 512 + half * 512:1024 + half * 512]),
                gates=[ev_guard[dc % 2]])
            bank = [ps_a, ps_b][dc % 2]
            wait("tensor", ev_guard[dc % 2], ca, cb)
            if dc == 0:
                wait("tensor", a_last)
            for dnc in range(32):
                em("tensor", lambda te, l=wob[:, dnc * 128:(dnc + 1) * 128], r=ah[dnc][:, :],
                   b=bank, st=(dnc == 0), sp=(dnc == 31): te.matmul(b[:, :], l, r, start=st, stop=sp))
            pm = mark("pe")
            wopm[dc] = pm
            wait("vector", xtok, pm)
            if LN1_TAP and half == 0 and dc == 0:
                em("vector", lambda v, b=bank: v.tensor_copy(xmid_c[:, :], b[:, :]))
                tpw = mark("dve")
                S_misc.issue(lambda e: e.dma_start(out=tap_in[:, 0:512], in_=xmid_c[:, :]), gates=[tpw])
            if LN1_TAP and half == 0 and dc == 0:
                em("vector", lambda v, xx=xrb: v.tensor_copy(ln_var[:, :], xx[:, :]))
                tpx = mark("dve")
                S_misc.issue(lambda e: e.dma_start(out=tap_v[:, 0:512], in_=ln_var[:, :]), gates=[tpx])
            em("vector", lambda v, b=bank, xx=xrb, d=res1[dc]: v.tensor_tensor(
                d[:, :], b[:, :], xx[:, :], ALU.add))
            rv = mark("dve")
            ev_guard[dc % 2] = rv
            if LN1_TAP and half == 0 and dc == 0:
                tearly = S_misc.issue(lambda e: e.dma_start(out=tap_bf[:, 1024:2048].bitcast(F32)[:, 0:512]
                                                            if False else tap_in[:, 512:1024], in_=res1[0][:, :]),
                                      gates=[rv])
                wait("vector", tearly)
                wait("tensor", tearly)
                wait("scalar", tearly)
            wait("scalar", rv, prev_stm[0])
            em("scalar", lambda sc, s=res1[dc]: sc.activation(out=sqt[:, :], in_=s[:, :], func=AF.Square))
            sqv = mark("act")
            wait("tensor", rv, sqv)
            if dc == 0 and half == 1:
                wait("tensor", stats_rd[0])
            em("tensor", lambda te, s=res1[dc], st=(dc == 0), sp=(dc == 31):
               te.matmul(st_a[:, :], ones_f[:, :], s[:, :], start=st, stop=sp))
            em("tensor", lambda te, st=(dc == 0), sp=(dc == 31):
               te.matmul(st_b[:, :], ones_f[:, :], sqt[:, :], start=st, stop=sp))
            stm = mark("pe")
            prev_stm[0] = stm
        wait("vector", stm)
        em("vector", lambda v: v.tensor_scalar(ln_mean[:, :], st_a[:, :], 1.0 / H, None, ALU.mult))
        em("vector", lambda v: v.tensor_scalar(ln_var[:, :], st_b[:, :], 1.0 / H, None, ALU.mult))
        stats_rd[0] = mark("dve")
        em("vector", lambda v: v.tensor_tensor(ln_b[:, :], ln_mean[:, :], ln_mean[:, :], ALU.mult))
        em("vector", lambda v: v.tensor_tensor(ln_var[:, :], ln_var[:, :], ln_b[:, :], ALU.subtract))
        sdone = mark("dve")
        em("vector", lambda v: v.tensor_scalar_add(ln_var[:, :], ln_var[:, :], EPS))
        sdone = mark("dve")
        wait("scalar", sdone)
        em("scalar", lambda sc: sc.activation(out=ln_std[:, :], in_=ln_var[:, :], func=AF.Sqrt))
        s2 = mark("act")
        wait("vector", s2)
        em("vector", lambda v: v.reciprocal(ln_rstd[:, :], ln_std[:, :]))
        em("vector", lambda v: v.scalar_tensor_tensor(ln_b[:, :], ln_mean[:, :], -1.0,
                                                      ln_rstd[:, :], ALU.mult, ALU.mult))
        mark("dve")
        if LN1_TAP and half == 0:
            t1 = S_misc.issue(lambda e: e.dma_start(out=tap_ps[:, :], in_=res1[0][:, :]),
                              gates=[curc("dve")])
            t2 = S_misc.issue(lambda e: e.dma_start(out=tap_q[:, :].bitcast(F32)[:, 0:256],
                                                    in_=ln_mean[:, 0:256]))
            t3 = S_misc.issue(lambda e: e.dma_start(out=tap_xg[:, :].bitcast(F32)[:, 0:256],
                                                    in_=ln_rstd[:, 0:256]))
            wait("vector", t3)
            wait("scalar", t3)
            wait("tensor", t3)
        xm_guard = None
        for dc in range(32):
            if xm_guard is not None:
                wait("vector", *xm_guard)
            em("vector", lambda v, s=res1[dc]: v.tensor_tensor(xmid_c[:, :], s[:, :], ln_rstd[:, :], ALU.mult))
            em("vector", lambda v: v.tensor_tensor(xmid_c[:, :], xmid_c[:, :], ln_b[:, :], ALU.add))
            xv = mark("dve")
            tok1 = S_xmw.issue(lambda e, dc=dc, half=half: e.dma_start(
                out=xmidf[dc, :, half * 512:(half + 1) * 512], in_=xmid_c[:, :]), gates=[xv])
            tok2 = S_xshw.issue(lambda e, dc=dc, half=half: e.dma_start(
                out=xstage[dc, :, half * 512:(half + 1) * 512], in_=xmid_c[:, :]), gates=[xv])
            xm_guard = (tok1, tok2)
            xsh_w_last = tok2
        prev_half_pe[0] = curc("pe")
        dq.pump(2, TB_23)

    p3_pe = curc("pe")
    p3_dve = curc("dve")
    p3_act = curc("act")
    xmw_last = xm_guard[0]

    if upto <= 3:
        dq.pump(1000, TB_23)
        wait("scalar", dq.store_tok[len(dq.units) - 1])
        wait("sync", xm_guard[0])
        wait("gpsimd", xsh_w_last)
        return _materialize(kb, nc)

    xcp = S_xshw.issue(lambda e: e.dma_start(out=xsh[kb.par["j"]], in_=xstage.ap()),
                       gates=[xsh_w_last])
    wait("gpsimd", xcp)
    bar_xmid = emit_pair_barrier()
    dq.pump(100, TB_23)
    assert dq.ui == len(dq.units)
    ag3a = emit_ag(slab3[0:16].rearrange("a b p f -> (a b p) f"),
                   gh3a.ap().rearrange("r a b p f -> (r a b p) f"),
                   [u_idx(2, dk) for dk in range(16)])
    ag3b = emit_ag(slab3[16:32].rearrange("a b p f -> (a b p) f"),
                   gh3b.ap().rearrange("r a b p f -> (r a b p) f"),
                   [u_idx(2, dk) for dk in range(16, 32)])

    if "tgh3a" in taps:
        tgh3a = dint("tgh3a", [4, 16, NLC, 128, 128], BF16)
        tgh3b = dint("tgh3b", [4, 16, NLC, 128, 128], BF16)
        wait("sync", ag3a, ag3b)
        tg3t = S_misc.issue(lambda e: e.dma_start(out=tgh3a.ap(), in_=gh3a.ap()))
        tg3t = S_misc.issue(lambda e: e.dma_start(out=tgh3b.ap(), in_=gh3b.ap()))
        wait("tensor", tg3t)

    def hc_list():
        out = []
        for lc in range(NLC):
            for gr in range(4):
                if gr == 3 and lc >= 10:
                    continue
                out.append((gr, lc))
        return out

    HCS = hc_list()
    assert len(HCS) == NHC

    pall_w_last = None
    prev_hg = [None]
    xm_rd_gate = [p3_pe, p3_dve, p3_act]
    w12_guard = {0: [p3_pe, p3_dve, p3_act], 1: [None]}
    hact_guard = {0: None, 1: None}
    dstage_rd = {0: None, 1: None}
    prev_down_pe = [None]
    ag12_gates = {True: ("sync_done", None)}
    first_gw = [True]
    for tg in range(4):
        slot = tg // 2
        hh = tg % 2
        wait("sync", bar_xmid)
        xm_toks = []
        for c in range(32):
            xm_toks.append(S_xmr.issue(lambda e, c=c, slot=slot, hh=hh: e.dma_start(
                out=xmg[c][:, :], in_=xsh[slot][c][:, hh * 512:(hh + 1) * 512]),
                gates=xm_rd_gate if c == 0 else ()))
        if FFN_TAP and tg == 0:
            tfx = S_misc.issue(lambda e: e.dma_start(out=tap_q[:, :], in_=xmg[0][:, :]),
                               gates=[xm_toks[-1]])
            wait("sync", ag12a2)
            S_misc.issue(lambda e: e.dma_start(out=tap_bf[:, 0:128].rearrange("p (a f) -> p a f", f=128),
                                               in_=ghw[(0, 0)][1, 0, 0:1].rearrange("a p f -> p a f")))
            S_misc.issue(lambda e: e.dma_start(out=tap_bf[:, 128:256].rearrange("p (a f) -> p a f", f=128),
                                               in_=ghw[(0, 0)][3, 2, 5:6].rearrange("a p f -> p a f")))
            wait("sync", ag3b)
            S_misc.issue(lambda e: e.dma_start(out=tap_bf[:, 256:384].rearrange("p (a f) -> p a f", f=128),
                                               in_=gh3a[1, 0, 0:1].rearrange("a p f -> p a f")))
            S_misc.issue(lambda e: e.dma_start(out=tap_bf[:, 384:512].rearrange("p (a f) -> p a f", f=128),
                                               in_=gh3b[3, 21 - 16, 2:3].rearrange("a p f -> p a f")))
            em("sync", lambda e, v=ag3a[1] if False else 0: e.nop())
            wait("tensor", tfx)
        for ci, (gr, lc) in enumerate(HCS):
            sl = ci % 2
            w1b = [w1t0, w1t1][sl]
            w2b = [w2t0, w2t1][sl]
            if first_gw[0] or (tg == 0 and lc >= 6 and (gr, lc) == (0, 6)):
                need = ag12a2 if first_gw[0] else ag12b2
                wait("sync", need)
                if tg == 0 and not first_gw[0]:
                    pass
                first_gw[0] = False
            gts = w12_guard[sl] if not isinstance(w12_guard[sl], list) else w12_guard[sl]
            t1 = S_gw.issue(lambda e, gr=gr, lc=lc, d=w1b: e.dma_start(
                out=d[:, :].rearrange("p (a f) -> p a f", f=128),
                in_=(ghw[(0, 0)][gr, lc] if lc < 6 else ghw[(0, 1)][gr, lc - 6]).rearrange("a p f -> p a f")),
                gates=gts if isinstance(gts, list) else [gts])
            t2 = S_gw.issue(lambda e, gr=gr, lc=lc, d=w2b: e.dma_start(
                out=d[:, :].rearrange("p (a f) -> p a f", f=128),
                in_=(ghw[(1, 0)][gr, lc] if lc < 6 else ghw[(1, 1)][gr, lc - 6]).rearrange("a p f -> p a f")))
            gbank = [ps_a, ps_b][sl]
            ubank = [ps_e[:, :], psc[:, 0:512]][sl]
            wait("tensor", hact_guard[sl], t2)
            if ci == 0:
                wait("tensor", xm_toks[-1])
                if tg == 0:
                    wait("tensor", p3_dve, p3_act)
            for dnc in range(32):
                em("tensor", lambda te, l=w1b[:, dnc * 128:(dnc + 1) * 128], r=xmg[dnc][:, :],
                   b=gbank, st=(dnc == 0), sp=(dnc == 31): te.matmul(b[:, :], l, r, start=st, stop=sp))
            gm = mark("pe")
            for dnc in range(32):
                em("tensor", lambda te, l=w2b[:, dnc * 128:(dnc + 1) * 128], r=xmg[dnc][:, :],
                   b=ubank, st=(dnc == 0), sp=(dnc == 31): te.matmul(b, l, r, start=st, stop=sp))
            um = mark("pe")
            wait("scalar", gm, prev_hg[0])
            em("scalar", lambda sc, b=gbank: sc.activation(out=silu_t[:, :], in_=b[:, :], func=AF.Silu))
            sv = mark("act")
            wait("vector", sv, um)
            if prev_down_pe[0] is not None and ci == 0:
                wait("vector", prev_down_pe[0])
            em("vector", lambda v, u=ubank, d=hact[ci]: v.tensor_tensor(d[:, :], silu_t[:, :], u, ALU.mult))
            hg = mark("dve")
            if FFN_TAP and tg == 0 and ci in (0, 5):
                S_misc.issue(lambda e, ci=ci: e.dma_start(
                    out=tap_vt[:, (0 if ci == 0 else 512):(512 if ci == 0 else 1024)].bitcast(BF16)[:, 0:512],
                    in_=hact[ci][:, :]), gates=[hg])
            hact_guard[sl] = hg
            w12_guard[sl] = hg
        hact_all = curc("dve")
        if FFN_TAP and tg == 0:
            htok = None
            for ci in range(NHC):
                htok = S_misc.issue(lambda e, ci=ci: e.dma_start(out=tap_hact[ci], in_=hact[ci][:, :]),
                                    gates=[hact_all if ci == 0 else None])
            wait("tensor", htok)
        w3_guard = {0: None, 1: None}
        wait("tensor", hact_all)
        for dc in range(32):
            sl = dc % 2
            w3b = [w3t0, w3t1][sl]
            if tg == 0 and dc == 0:
                wait("scalar", ag3a)
            if tg == 0 and dc == 16:
                wait("scalar", ag3b)
            toks = []
            for gr in range(4):
                toks.append(S_g3.issue(lambda e, gr=gr, dc=dc, d=w3b: e.dma_start(
                    out=d[:, (gr * NLC) * 128:((gr + 1) * NLC) * 128].rearrange("p (a f) -> p a f", f=128),
                    in_=(gh3a[gr, dc] if dc < 16 else gh3b[gr, dc - 16]).rearrange("a p f -> p a f")),
                    gates=[w3_guard[sl] if gr == 0 else None,
                           (p3_act if (tg == 0 and dc == 0 and gr == 0) else None)]))
            bank = [ps_a, ps_b][sl]
            wait("tensor", dstage_rd[sl], toks[-1])
            if FFN_TAP and tg == 0 and dc == 0:
                S_misc.issue(lambda e, d=w3b: e.dma_start(out=tap_bf[:, 0:4096],
                                                          in_=d[:, 0:4096]), gates=[toks[-1]])
                S_misc.issue(lambda e, d=w3b: e.dma_start(
                    out=tap_hact[0:32, :, 0:128].rearrange("a p f -> p a f"),
                    in_=d[:, 12 * 128:44 * 128].rearrange("p (a f) -> p a f", f=128)))
            first = True
            for ci, (gr, lc) in enumerate(HCS):
                em("tensor", lambda te, l=w3b[:, (gr * NLC + lc) * 128:(gr * NLC + lc + 1) * 128],
                   r=hact[ci][:, :], b=bank, st=first, sp=(ci == NHC - 1):
                   te.matmul(b[:, :], l, r, start=st, stop=sp))
                first = False
            dm = mark("pe")
            w3_guard[sl] = dm
            if FFN_TAP and tg == 0 and dc == 0:
                wait("vector", dm)
                em("vector", lambda v, b=bank: v.tensor_copy(silu_t[:, :], b[:, :]))
                tdp = mark("dve")
                S_misc.issue(lambda e: e.dma_start(out=tap_ps[:, :], in_=silu_t[:, :]), gates=[tdp])
                wait("scalar", tdp)
            wait("scalar", dm)
            stg = [dstage0, dstage1][sl]
            wait("scalar", dstage_rd[sl])
            em("scalar", lambda sc, b=bank, d=stg: sc.activation(out=d[:, :], in_=b[:, :], func=AF.Copy))
            av = mark("act")
            tok = S_pw.issue(lambda e, dc=dc, slot=slot, hh=hh, s=stg: e.dma_start(
                out=pstage[slot, dc, :, hh * 512:(hh + 1) * 512], in_=s[:, :]), gates=[av])
            pall_w_last = tok
            dstage_rd[sl] = tok
        prev_down_pe[0] = curc("pe")
        xm_rd_gate = [curc("pe")]

    if upto <= 4:
        wait("gpsimd", pall_w_last)
        return _materialize(kb, nc)
    pcp0 = S_pw.issue(lambda e: e.dma_start(out=pall[kb.par["j"], 0], in_=pstage[0]),
                      gates=[pall_w_last])
    pcp1 = S_pw.issue(lambda e: e.dma_start(out=pall[kb.par["j"], 1], in_=pstage[1]))
    wait("gpsimd", pcp1)
    bar_pall = emit_pair_barrier()
    p4_pe = curc("pe")
    p4_dve = curc("dve")
    p4_act = curc("act")

    out_last = None
    pg = [[p4_pe, p4_dve, p4_act]]
    wait("gpsimd", bar_pall)
    prcp0 = S_pr.issue(lambda e: e.dma_start(out=prstage[0], in_=pall[0, kb.par["j"]]))
    prcp1 = S_pr.issue(lambda e: e.dma_start(out=prstage[1], in_=pall[1, kb.par["j"]]))
    l2_stats_rd = [None]
    for half in range(2):
        stm2 = None
        prev_stm2 = [None]
        for dc in range(32):
            tok0 = S_pr.issue(lambda e, dc=dc, half=half: e.dma_start(
                out=p_in0[:, :], in_=prstage[0, dc, :, half * 512:(half + 1) * 512]),
                gates=pg[0] + ([prcp1] if (half == 0 and dc == 0) else []))
            tok1 = S_pr.issue(lambda e, dc=dc, half=half: e.dma_start(
                out=p_in1[:, :], in_=prstage[1, dc, :, half * 512:(half + 1) * 512]))
            tok2 = S_xfr.issue(lambda e, dc=dc, half=half: e.dma_start(
                out=p_in2[:, :], in_=xmidf[dc, :, half * 512:(half + 1) * 512]),
                gates=[xmw_last if (half == 0 and dc == 0) else None] + pg[0])
            wait("vector", tok0, tok1, tok2)
            em("vector", lambda v, d=res2[dc]: v.tensor_tensor(d[:, :], p_in0[:, :], p_in1[:, :], ALU.add))
            em("vector", lambda v, d=res2[dc]: v.tensor_tensor(d[:, :], d[:, :], p_in2[:, :], ALU.add))
            rv = mark("dve")
            pg[0] = [rv]
            wait("scalar", rv, prev_stm2[0])
            em("scalar", lambda sc, s=res2[dc]: sc.activation(out=l2_sq[:, :], in_=s[:, :], func=AF.Square))
            sv = mark("act")
            wait("tensor", rv, sv)
            if dc == 0 and half == 1:
                wait("tensor", l2_stats_rd[0])
            em("tensor", lambda te, s=res2[dc], st=(dc == 0), sp=(dc == 31):
               te.matmul(st_a[:, :], ones_f[:, :], s[:, :], start=st, stop=sp))
            em("tensor", lambda te, st=(dc == 0), sp=(dc == 31):
               te.matmul(st_b[:, :], ones_f[:, :], l2_sq[:, :], start=st, stop=sp))
            stm2 = mark("pe")
            prev_stm2[0] = stm2
        wait("vector", stm2)
        em("vector", lambda v: v.tensor_scalar(l2_mean[:, :], st_a[:, :], 1.0 / H, None, ALU.mult))
        em("vector", lambda v: v.tensor_scalar(l2_var[:, :], st_b[:, :], 1.0 / H, None, ALU.mult))
        l2_stats_rd[0] = mark("dve")
        em("vector", lambda v: v.tensor_tensor(l2_b[:, :], l2_mean[:, :], l2_mean[:, :], ALU.mult))
        em("vector", lambda v: v.tensor_tensor(l2_var[:, :], l2_var[:, :], l2_b[:, :], ALU.subtract))
        sdone = mark("dve")
        em("vector", lambda v: v.tensor_scalar_add(l2_var[:, :], l2_var[:, :], EPS))
        sdone = mark("dve")
        wait("scalar", sdone)
        em("scalar", lambda sc: sc.activation(out=l2_std[:, :], in_=l2_var[:, :], func=AF.Sqrt))
        s2m = mark("act")
        wait("vector", s2m)
        em("vector", lambda v: v.reciprocal(l2_rstd[:, :], l2_std[:, :]))
        em("vector", lambda v: v.scalar_tensor_tensor(l2_b[:, :], l2_mean[:, :], -1.0,
                                                      l2_rstd[:, :], ALU.mult, ALU.mult))
        mark("dve")
        tguard = {0: None, 1: None}
        nvg = None
        for dc in range(32):
            if nvg is not None:
                wait("vector", nvg)
            em("vector", lambda v, s=res2[dc]: v.tensor_tensor(onorm[:, :], s[:, :], l2_rstd[:, :], ALU.mult))
            em("vector", lambda v: v.tensor_tensor(onorm[:, :], onorm[:, :], l2_b[:, :], ALU.add))
            nv = mark("dve")
            wait("tensor", nv)
            if out_last is not None:
                wait("scalar", out_last)
            for tt in range(4):
                sl = tt % 2
                bap = [ps_e[:, 0:128], psc[:, 0:128]][sl]
                wait("tensor", tguard[sl])
                em("tensor", lambda te, b=bap, s=onorm[:, tt * 128:(tt + 1) * 128]:
                   te.transpose(b, s, ident_f[:, :]))
                tm = mark("pe")
                wait("scalar", tm)
                em("scalar", lambda sc, b=bap, d=ostage[:, tt * 128:(tt + 1) * 128]:
                   sc.activation(out=d, in_=b, func=AF.Copy))
                tguard[sl] = mark("act")
            ost = curc("act")
            nvg = curc("pe")
            for tt in range(4):
                out_last = S_out.issue(lambda e, dc=dc, half=half, tt=tt: e.dma_start(
                    out=y[half * 512 + tt * 128:half * 512 + (tt + 1) * 128, dc * 128:(dc + 1) * 128],
                    in_=ostage[:, tt * 128:(tt + 1) * 128]), gates=[ost if tt == 0 else None])
    wait("scalar", out_last)
    return _materialize(kb, nc)


def _materialize(kb, nc):
    with nc.Block() as block:
        @block.sync
        def _(e):
            for f in kb.ops["sync"]:
                f(e)

        @block.scalar
        def _(e):
            for f in kb.ops["scalar"]:
                f(e)

        @block.vector
        def _(e):
            for f in kb.ops["vector"]:
                f(e)

        @block.tensor
        def _(e):
            for f in kb.ops["tensor"]:
                f(e)

        @block.gpsimd
        def _(e):
            for f in kb.ops["gpsimd"]:
                f(e)

    return nc


def _host_inputs(x, Wq, Wk, Wv, Wo, w1, w2, w3):
    ins = []
    Wq_t = np.ascontiguousarray(Wq.reshape(H, 32, 128).transpose(1, 0, 2))
    Wk_t = np.ascontiguousarray(Wk.reshape(H, 8, 128).transpose(1, 0, 2))
    Wv_t = np.ascontiguousarray(Wv.reshape(H, 8, 128).transpose(1, 0, 2))
    Wo_t = np.ascontiguousarray(Wo.reshape(H, 32, 128).transpose(1, 0, 2))
    idbf = np.eye(128, dtype=bfnp)
    idf = np.eye(128, dtype=np.float32)
    onesf = np.ones((128, 128), dtype=np.float32)
    invf = (1.0 / (10000.0 ** (np.arange(64, dtype=np.float64) / 64.0)))
    for c in range(8):
        b, chunk = divmod(c, 4)
        q0 = chunk * T
        xt = np.zeros((KT, H), dtype=np.float32)
        lo = q0 - 512
        v0 = max(0, lo)
        xt[v0 - lo:, :] = x[b, v0:q0 + T, :]
        xT = np.ascontiguousarray(xt.T)
        posq = np.arange(q0, q0 + T, dtype=np.float64)
        posk = np.clip(np.arange(lo, q0 + T, dtype=np.float64), 0, None)
        angq = posq[None, :] * invf[:, None]
        angk = posk[None, :] * invf[:, None]
        sc = 1.0 / np.sqrt(HD)
        cosq_ = (np.concatenate([np.cos(angq), np.cos(angq)], 0) * sc).astype(bfnp)
        sinq_ = (np.concatenate([-np.sin(angq), np.sin(angq)], 0) * sc).astype(bfnp)
        cosk_ = np.concatenate([np.cos(angk), np.cos(angk)], 0).astype(bfnp)
        sink_ = np.concatenate([-np.sin(angk), np.sin(angk)], 0).astype(bfnp)
        mk = np.zeros((8, 128, 640), dtype=bfnp)
        qr = np.arange(128)[:, None]
        kr = np.arange(640)[None, :]
        band = (kr >= qr) & (kr <= qr + 512)
        for qt in range(8):
            kabs = q0 + qt * 128 - 512 + kr
            ok = band & (kabs >= 0)
            mk[qt] = np.where(ok, np.float32(0.0), np.float32(-1e30)).astype(bfnp)
        g = c // 2
        j = c % 2
        hbase = j * HHALF + QOFF[g]
        qlen = QL[g]
        w1q = np.zeros((1408, H), dtype=np.float32)
        w2q = np.zeros((1408, H), dtype=np.float32)
        w3q = np.zeros((H, 2048), dtype=np.float32)
        w1q[:qlen] = w1[hbase:hbase + qlen, :]
        w2q[:qlen] = w2[hbase:hbase + qlen, :]
        w3q[:, :qlen] = w3[:, hbase:hbase + qlen]
        ins.append({
            "xT": xT, "cosq": cosq_, "sinq": sinq_, "cosk": cosk_, "sink": sink_,
            "masks": mk, "idbf": idbf, "idf": idf, "onesf": onesf,
            "Wq": Wq_t, "Wk": Wk_t, "Wv": Wv_t, "Wo": Wo_t,
            "w1q": np.ascontiguousarray(w1q), "w2q": np.ascontiguousarray(w2q),
            "w3q": np.ascontiguousarray(w3q),
        })
    return ins


_NC_CACHE = {}


def kernel(L0_attn_hidden_states, L0_attn_Wq, L0_attn_Wk, L0_attn_Wv, L0_attn_Wo,
           L0_ffn_w1, L0_ffn_w2, L0_ffn_w3):
    x = np.asarray(L0_attn_hidden_states, dtype=np.float32)
    ins = _host_inputs(x, np.asarray(L0_attn_Wq, np.float32), np.asarray(L0_attn_Wk, np.float32),
                       np.asarray(L0_attn_Wv, np.float32), np.asarray(L0_attn_Wo, np.float32),
                       np.asarray(L0_ffn_w1, np.float32), np.asarray(L0_ffn_w2, np.float32),
                       np.asarray(L0_ffn_w3, np.float32))
    if "nc" not in _NC_CACHE:
        _NC_CACHE["nc"] = build()
    nc = _NC_CACHE["nc"]
    res = run_bass_kernel_spmd(nc, ins, core_ids=list(range(8)))
    out = np.empty((B, S, H), dtype=np.float32)
    for c in range(8):
        b, chunk = divmod(c, 4)
        out[b, chunk * T:(chunk + 1) * T, :] = res.results[c]["y"]
    return out
